# version 12
# baseline (speedup 1.0000x reference)
import numpy as np
import ml_dtypes
import concourse.bass as bass
import concourse.mybir as mybir
from concourse.bass_utils import run_bass_kernel_spmd
from contextlib import ExitStack

F32 = mybir.dt.float32
BF16 = mybir.dt.bfloat16
AF = mybir.ActivationFunctionType
ALU = mybir.AluOpType
AX = mybir.AxisListType

B, S, E, H, D = 4, 4096, 1024, 16, 64
NC_ = 8
T = 2048
NCH = 16
NTI = 4
EPS = 1e-5
EPSZ = 1e-6
RC = 12582912.0
QMAX = 127.0
THRESH = 0.7


def build_nc(debug=False):
    nc = bass.Bass("TRN2", target_bir_lowering=False,
                   dynamic_dma_scratch_size=8192)

    x_d = nc.dram_tensor("x", [T, E], F32, kind="ExternalInput")
    wqkv_d = nc.dram_tensor("wqkv", [E, 3 * E], F32, kind="ExternalInput")
    bqkv_d = nc.dram_tensor("bqkv", [3 * E], F32, kind="ExternalInput")
    wffn_d = nc.dram_tensor("wffn", [E, E], F32, kind="ExternalInput")
    gam_d = nc.dram_tensor("gamma", [E], F32, kind="ExternalInput")
    bet_d = nc.dram_tensor("beta", [E], F32, kind="ExternalInput")
    out_d = nc.dram_tensor("out", [T, E], F32, kind="ExternalOutput")

    _dbgkind = "ExternalOutput" if debug else "Internal"
    attn_dram = nc.dram_tensor("attn_dram", [E, T], BF16, kind=_dbgkind)
    xq_dram = nc.dram_tensor("xq_dram", [T, E], BF16, kind=_dbgkind)
    wq_dram = nc.dram_tensor("wq_dram", [E, E], BF16)
    cc_kv_i = nc.dram_tensor("cc_kv_i", [128, 520], F32)
    cc_kv_o = nc.dram_tensor("cc_kv_o", [128, 520], F32)
    cc_mx_i = nc.dram_tensor("cc_mx_i", [1, 1], F32)
    cc_mx_o = nc.dram_tensor("cc_mx_o", [1, 1], F32)
    scr = nc.dram_tensor("scr", [1, 4], F32)
    if debug:
        dbg_kvloc = nc.dram_tensor("dbg_kvloc", [128, 520], F32, kind="ExternalOutput")
        dbg_z = nc.dram_tensor("dbg_z", [128, 256], F32, kind="ExternalOutput")
        dbg_sc = nc.dram_tensor("dbg_sc", [128, 16], F32, kind="ExternalOutput")

    with ExitStack() as P:
        e = P.enter_context
        r_sb = e(nc.sbuf_tensor("r_sb", [128, NCH, E], F32))
        actT = e(nc.sbuf_tensor("actT", [128, 8, T], BF16))
        kv_full = e(nc.sbuf_tensor("kv_full", [128, 520], F32))
        bd = e(nc.sbuf_tensor("bd", [128, 8, 128], BF16))
        ksbd = e(nc.sbuf_tensor("ksbd", [128, 16], BF16))
        z_all = e(nc.sbuf_tensor("z_all", [128, NCH, H], F32))
        mu_all = e(nc.sbuf_tensor("mu_all", [128, NCH], F32))
        var_all = e(nc.sbuf_tensor("var_all", [128, NCH], F32))
        mvp = e(nc.sbuf_tensor("mvp", [128, 2], F32))
        st12 = e(nc.sbuf_tensor("st12", [128, 12], F32))
        rstd_all = e(nc.sbuf_tensor("rstd_all", [128, NCH], F32))
        nm_all = e(nc.sbuf_tensor("nm_all", [128, NCH], F32))
        s2_all = e(nc.sbuf_tensor("s2_all", [128, NCH], F32))
        ve4 = e(nc.sbuf_tensor("ve4", [128, 4], F32))
        y04 = e(nc.sbuf_tensor("y04", [128, 4], F32))
        t4 = e(nc.sbuf_tensor("t4", [128, 4], F32))
        var2 = e(nc.sbuf_tensor("var2", [128, 4], F32))
        y04b = e(nc.sbuf_tensor("y04b", [128, 4], F32))
        t4b = e(nc.sbuf_tensor("t4b", [128, 4], F32))
        bq_sb = e(nc.sbuf_tensor("bq_sb", [128, 8], F32))
        ones_row = e(nc.sbuf_tensor("ones_row", [1, 128], BF16))
        ones_cb = e(nc.sbuf_tensor("ones_cb", [128, 1], BF16))
        ones_cf = e(nc.sbuf_tensor("ones_cf", [128, 1], F32))
        eps_c = e(nc.sbuf_tensor("eps_c", [128, 1], F32))
        id_f32 = e(nc.sbuf_tensor("id_f32", [128, 128], F32))
        id_b = e(nc.sbuf_tensor("id_b", [128, 128], BF16))
        absacc = e(nc.sbuf_tensor("absacc", [128, 8], F32))
        numacc = e(nc.sbuf_tensor("numacc", [128, 8], F32))
        denacc = e(nc.sbuf_tensor("denacc", [128, 8], F32))
        s1 = e(nc.sbuf_tensor("s1", [1, 1], F32))
        dnum = e(nc.sbuf_tensor("dnum", [1, 1], F32))
        dden = e(nc.sbuf_tensor("dden", [1, 1], F32))
        alpha = e(nc.sbuf_tensor("alpha", [1, 1], F32))
        delta_c = e(nc.sbuf_tensor("delta_c", [128, 1], F32))
        ndelta_c = e(nc.sbuf_tensor("ndelta_c", [128, 1], F32))
        mxcols = e(nc.sbuf_tensor("mxcols", [128, NCH], F32))
        mxcol = e(nc.sbuf_tensor("mxcol", [128, 1], F32))
        mxs = e(nc.sbuf_tensor("mxs", [1, 1], F32))
        gmax = e(nc.sbuf_tensor("gmax", [1, 1], F32))
        scpair = e(nc.sbuf_tensor("scpair", [1, 2], F32))
        scb = e(nc.sbuf_tensor("scb", [128, 2], F32))
        t_b = e(nc.sbuf_tensor("t_b", [128, 1], F32))
        rc_c = e(nc.sbuf_tensor("rc_c", [128, 1], F32))
        nrc_c = e(nc.sbuf_tensor("nrc_c", [128, 1], F32))
        alpha_b = e(nc.sbuf_tensor("alpha_b", [128, 1], F32))
        gmax_b = e(nc.sbuf_tensor("gmax_b", [128, 1], F32))

        _ws = e(nc.sbuf_tensor("wstage0", [128, E], F32))
        wstage = [_ws, _ws]
        wdum = e(nc.sbuf_tensor("wdum", [128, E], BF16))

        names = ["dX0", "dX1", "dX2", "dX3", "dWs", "dWg", "gMK",
                 "dWgQ", "sSGN", "dMK0", "dMK1", "sRD", "dAB", "dGM",
                 "sATB", "sACC",
                 "sTP", "vTE", "vIN", "gIN",
                 "vST", "sLN", "vNW", "sSX",
                 "sTQ", "sSQ", "sVQ", "sTK", "sTV", "sSK", "sVK", "sVV", "sTM2",
                 "vKL", "dKVI", "sCC", "dKVO", "gKV",
                 "dWF0", "dWF1", "vWS", "gRD", "vDL", "dSCR", "dDEL",
                 "vMSK", "dWQ", "gRD2", "vAL", "dWQT",
                 "bar1", "vBD", "sTD", "sTM3", "sSAT",
                 "dAT0", "dAT1", "dAT2", "dAT3",
                 "dATT0", "dATT1", "dATT2", "dATT3",
                 "vDE", "sDE", "sTT", "vZ", "vSC",
                 "bar2", "dGB", "vL2", "sL2", "vNW2", "sY2", "gGM", "vBB",
                 "vMX", "gMX", "dMXI", "sCC2", "dMXO", "vSCL", "dSCB",
                 "vRD", "dXQ0", "dXQ1", "dXQT0", "dXQT1", "dXQT2", "dXQT3",
                 "sTM4", "vEV", "dOUT"]
        sem = {n: e(nc.semaphore(n)) for n in names}

        with ExitStack() as PA:
            ea = PA.enter_context
            wq_sb = ea(nc.sbuf_tensor("wq_sb", [128, 8, 3 * E], BF16))
            xn1T_f = ea(nc.sbuf_tensor("xn1T_f", [128, 8, T], BF16))
            xn1_tm = [ea(nc.sbuf_tensor(f"xn1_tm{j}", [128, E], BF16)) for j in range(2)]
            pk_tm = ea(nc.sbuf_tensor("pk_tm", [128, E], BF16))
            v_tm = ea(nc.sbuf_tensor("v_tm", [128, E], BF16))
            ek_b = ea(nc.sbuf_tensor("ek_b", [128, E], BF16))
            rk_b = ea(nc.sbuf_tensor("rk_b", [128, E], BF16))
            eq_b = ea(nc.sbuf_tensor("eq_b", [128, 512], BF16))
            rq_b = ea(nc.sbuf_tensor("rq_b", [128, 512], BF16))
            bkv_row = ea(nc.sbuf_tensor("bkv_row", [1, 2 * E], BF16))
            kv_loc = ea(nc.sbuf_tensor("kv_loc", [128, 520], F32))
            kvps = ea(nc.psum_tensor("kvps", [128, 512], F32))
            ksps = ea(nc.psum_tensor("ksps", [128, 8], F32))
            ptr = [ea(nc.psum_tensor(f"ptr{j}", [128, 512], BF16)) for j in range(2)]
            kps = ea(nc.psum_tensor("kps", [128, 1024], F32))
            vps = ea(nc.psum_tensor("vps", [128, 1024], F32))
            qps = [kps[:, 0:512], kps[:, 512:1024]]

            with nc.Block() as blk:

                @blk.sync
                def _(sync):
                    for i in range(NTI):
                        sync.dma_start(
                            r_sb[:, 4 * i:4 * i + 4, :],
                            x_d.ap().rearrange("(c p) f -> p c f", p=128)[:, 4 * i:4 * i + 4, :],
                        ).then_inc(sem[f"dX{i}"], 16)
                    sync.dma_start(bq_sb[:, :],
                                   bass.AP(bqkv_d, 0, [[1, 128], [128, 8]]),
                                   allow_slow_non_contiguous=True,
                                   ).then_inc(sem["dWs"], 16)
                    for _i in range(NTI):
                        sync.wait_ge(sem[f"dX{_i}"], 16)

                @blk.gpsimd
                def _(gpsimd):
                    gpsimd.memset(id_b[:, :], 1.0).then_inc(sem["gIN"], 1)
                    gpsimd.wait_ge(sem["gIN"], 1)
                    gpsimd.affine_select(id_b[:, :], id_b[:, :], [[-1, 128]],
                                         ALU.is_equal, 0.0,
                                         channel_multiplier=1).then_inc(sem["gIN"], 1)
                    gpsimd.memset(id_f32[:, :], 1.0).then_inc(sem["gIN"], 1)
                    gpsimd.wait_ge(sem["gIN"], 3)
                    gpsimd.affine_select(id_f32[:, :], id_f32[:, :], [[-1, 128]],
                                         ALU.is_equal, 0.0,
                                         channel_multiplier=1).then_inc(sem["gIN"], 1)
                    gpsimd.dma_start(
                        bkv_row[0:1, :],
                        bass.AP(bqkv_d, E, [[2 * E, 1], [1, 2 * E]]),
                    ).then_inc(sem["dWg"], 16)
                    for k in range(8):
                        gpsimd.dma_start(
                            wq_sb[:, k, E:3 * E],
                            wqkv_d[k * 128:(k + 1) * 128, E:3 * E]
                        ).then_inc(sem["dWg"], 16)
                    for k in range(8):
                        gpsimd.dma_start(
                            wq_sb[:, k, 0:E],
                            wqkv_d[k * 128:(k + 1) * 128, 0:E]
                        ).then_inc(sem["dWgQ"], 16)
                    gpsimd.wait_ge(sem["vKL"], 2)
                    gpsimd.dma_start(cc_kv_i[0:128, :], kv_loc[:, :]).then_inc(sem["dKVI"], 16)
                    gpsimd.wait_ge(sem["dKVI"], 16)
                    gpsimd.collective_compute(
                        "AllReduce", ALU.add,
                        replica_groups=[[0, 1], [2, 3], [4, 5], [6, 7]],
                        ins=[cc_kv_i.ap().opt()],
                        outs=[cc_kv_o.ap().opt()],
                    ).then_inc(sem["sCC"], 1)
                    gpsimd.wait_ge(sem["vWS"], 8)
                    gpsimd.tensor_reduce(s1[0:1, 0:1], absacc[:, :], AX.XYZWC,
                                         ALU.add).then_inc(sem["gRD"], 1)
                    gpsimd.wait_ge(sem["gRD"], 1)
                    gpsimd.tensor_scalar_mul(s1[0:1, 0:1], s1[0:1, 0:1],
                                             THRESH / (E * E)).then_inc(sem["vDL"], 1)
                    gpsimd.wait_ge(sem["vDL"], 1)
                    gpsimd.dma_start(scr[0:1, 0:1], s1[0:1, 0:1]).then_inc(sem["dSCR"], 16)
                    gpsimd.wait_ge(sem["dSCR"], 16)
                    gpsimd.dma_start(delta_c[:, :],
                                     bass.AP(scr, 0, [[0, 128], [1, 1]])
                                     ).then_inc(sem["dDEL"], 16)
                    gpsimd.wait_ge(sem["dDEL"], 16)
                    gpsimd.tensor_scalar_mul(ndelta_c[:, :], delta_c[:, :],
                                             -1.0).then_inc(sem["vDL"], 1)
                    gpsimd.wait_ge(sem["sCC"], 1)
                    gpsimd.dma_start(kv_full[:, :], cc_kv_o[0:128, :]).then_inc(sem["dKVO"], 16)
                    gpsimd.wait_ge(sem["dKVO"], 16)
                    gpsimd.sem_inc(sem["gKV"], 1)

                @blk.vector
                def _(vector):
                    cv = 0
                    vector.memset(ones_row[:, :], 1.0)
                    vector.memset(ones_cb[:, :], 1.0)
                    vector.memset(ones_cf[:, :], 1.0)
                    vector.memset(eps_c[:, :], EPS)
                    vector.memset(rc_c[:, :], RC)
                    vector.memset(nrc_c[:, :], -RC)
                    vector.sem_inc(sem["vIN"], 1)
                    nw = 0

                    def ln1_chunk(g):
                        nonlocal cv
                        i = g // 4
                        vector.wait_ge(sem[f"dX{i}"], 16)
                        if g > 0:
                            vector.wait_ge(sem["vST"], cv)
                        vector.bn_stats(st12[:, 0:6], r_sb[:, g, 0:512]).then_inc(sem["vST"], 1)
                        vector.bn_stats(st12[:, 6:12], r_sb[:, g, 512:1024]).then_inc(sem["vST"], 1)
                        cv += 2
                        vector.wait_ge(sem["vST"], cv)
                        vector.bn_aggr(mvp[:, :], st12[:, :]).then_inc(sem["vST"], 1)
                        cv += 1
                        vector.wait_ge(sem["vST"], cv)
                        vector.tensor_copy(mu_all[:, g:g + 1], mvp[:, 0:1]).then_inc(sem["vST"], 1)
                        vector.tensor_copy(var_all[:, g:g + 1], mvp[:, 1:2]).then_inc(sem["vST"], 1)
                        cv += 2
                        assert cv == 5 * (g + 1)

                    def ln1_newton(i):
                        nonlocal cv, nw
                        vector.wait_ge(sem["vST"], cv)
                        vector.tensor_scalar_add(
                            ve4[:, :], var_all[:, 4 * i:4 * i + 4], EPS
                        ).then_inc(sem["vNW"], 1)
                        vector.wait_ge(sem["sLN"], 2 * (i + 1))
                        vector.wait_ge(sem["vNW"], nw + 1)
                        vector.tensor_mul(t4[:, :], y04[:, :], y04[:, :]).then_inc(sem["vNW"], 1)
                        vector.wait_ge(sem["vNW"], nw + 2)
                        vector.tensor_mul(t4[:, :], t4[:, :], ve4[:, :]).then_inc(sem["vNW"], 1)
                        vector.wait_ge(sem["vNW"], nw + 3)
                        vector.tensor_scalar(t4[:, :], t4[:, :], -0.5, 1.5,
                                             ALU.mult, ALU.add).then_inc(sem["vNW"], 1)
                        vector.wait_ge(sem["vNW"], nw + 4)
                        vector.tensor_mul(rstd_all[:, 4 * i:4 * i + 4],
                                          y04[:, :], t4[:, :]).then_inc(sem["vNW"], 1)
                        vector.wait_ge(sem["vNW"], nw + 5)
                        vector.tensor_mul(nm_all[:, 4 * i:4 * i + 4],
                                          mu_all[:, 4 * i:4 * i + 4],
                                          rstd_all[:, 4 * i:4 * i + 4]).then_inc(sem["vNW"], 1)
                        vector.wait_ge(sem["vNW"], nw + 6)
                        vector.tensor_scalar_mul(nm_all[:, 4 * i:4 * i + 4],
                                                 nm_all[:, 4 * i:4 * i + 4],
                                                 -1.0).then_inc(sem["vNW"], 1)
                        nw += 7

                    def ln1_tile(i):
                        for c in range(4):
                            ln1_chunk(4 * i + c)
                        ln1_newton(i)

                    def kphi_tile(i, cb=None, pre=None):
                        for c in range(4):
                            g = 4 * i + c
                            if pre is not None:
                                pre(c)
                            vector.wait_ge(sem["sSK"], 2 * (g + 1))
                            if g > 0:
                                vector.wait_ge(sem["sTM2"], g)
                            vector.scalar_tensor_tensor(
                                pk_tm[:, :], ek_b[:, :], ones_cf[:, 0:1], rk_b[:, :],
                                ALU.min, ALU.add,
                            ).then_inc(sem["sVK"], 1)
                            vector.wait_ge(sem["sTV"], g + 1)
                            vector.tensor_copy(v_tm[:, :], vps[:, :]).then_inc(sem["sVV"], 1)
                            if cb is not None:
                                cb(c)

                    def qphi_tile(i):
                        for m in range(8):
                            gm = 8 * i + m
                            vector.wait_ge(sem["sSQ"], 2 * gm + 2)
                            vector.scalar_tensor_tensor(
                                actT[:, m, i * 512:(i + 1) * 512],
                                eq_b[:, :], ones_cf[:, 0:1], rq_b[:, :],
                                ALU.min, ALU.add,
                            ).then_inc(sem["sVQ"], 1)

                    def kv_evac():
                        vector.wait_ge(sem["sTM2"], 16)
                        vector.tensor_copy(kv_loc[:, 0:512], kvps[:, :]).then_inc(sem["vKL"], 1)
                        vector.tensor_copy(kv_loc[:, 512:520], ksps[:, :]).then_inc(sem["vKL"], 1)

                    def evac_chunk(g):
                        for j in range(2):
                            vector.wait_ge(sem["sTP"], 2 * g + j + 1)
                            vector.tensor_copy(
                                bass.AP(xn1T_f, (4 * j) * T + g * 128,
                                        [[8 * T, 128], [T, 4], [1, 128]]),
                                ptr[j][:, :],
                            ).then_inc(sem["vTE"], 1)

                    ln1_tile(0)
                    evac_chunk(0)
                    ln1_tile(1)
                    evac_chunk(1)
                    for i in range(NTI):
                        def pre(c, i=i):
                            g = 4 * i + c
                            if 2 <= g + 1 < NCH:
                                evac_chunk(g + 1)

                        def cb(c, i=i):
                            if i + 2 < NTI:
                                ln1_chunk(4 * (i + 2) + c)
                                if c == 3:
                                    ln1_newton(i + 2)
                        kphi_tile(i, cb, pre)
                    kv_evac()
                    for i in range(NTI):
                        qphi_tile(i)

                @blk.scalar
                def _(scalar):

                    def seed_tile(i):
                        scalar.wait_ge(sem["vST"], 5 * (4 * i + 4))
                        if i > 0:
                            scalar.wait_ge(sem["vNW"], 7 * i)
                        scalar.wait_ge(sem["vIN"], 1)
                        scalar.activation(y04[:, :], var_all[:, 4 * i:4 * i + 4],
                                          AF.Ln, bias=eps_c[:, 0:1]).then_inc(sem["sLN"], 1)
                        scalar.wait_ge(sem["sLN"], 2 * i + 1)
                        scalar.activation(y04[:, :], y04[:, :], AF.Exp,
                                          scale=-0.5).then_inc(sem["sLN"], 1)

                    def xn1_chunk(i, c):
                        g = 4 * i + c
                        scalar.wait_ge(sem["vNW"], 7 * (i + 1))
                        if g >= 2:
                            scalar.wait_ge(sem["sTP"], 2 * (g - 1))
                        scalar.activation(xn1_tm[g % 2][:, :], r_sb[:, g, :], AF.Identity,
                                          bias=nm_all[:, g:g + 1],
                                          scale=rstd_all[:, g:g + 1]).then_inc(sem["sSX"], 1)

                    def skphi_tile(i, cb=None, cb0=0):
                        for c in range(4):
                            g = 4 * i + c
                            scalar.wait_ge(sem["sTK"], g + 1)
                            if g > 0:
                                scalar.wait_ge(sem["sVK"], g)
                            scalar.activation(ek_b[:, :], kps[:, :], AF.Exp).then_inc(sem["sSK"], 1)
                            scalar.activation(rk_b[:, :], kps[:, :], AF.Relu).then_inc(sem["sSK"], 1)
                            if cb is not None:
                                cb(cb0 + c)

                    def sqphi_tile(i):
                        if i == 0:
                            scalar.wait_ge(sem["dWs"], 16)
                        for m in range(8):
                            gm = 8 * i + m
                            scalar.wait_ge(sem["sTQ"], gm + 1)
                            if gm > 0:
                                scalar.wait_ge(sem["sVQ"], gm)
                            scalar.activation(eq_b[:, :], qps[m % 2][:, :], AF.Exp,
                                              bias=bq_sb[:, m:m + 1]).then_inc(sem["sSQ"], 1)
                            scalar.activation(rq_b[:, :], qps[m % 2][:, :], AF.Relu,
                                              bias=bq_sb[:, m:m + 1]).then_inc(sem["sSQ"], 1)

                    def stats_step(j):
                        scalar.wait_ge(sem["dWF1"], 16 * (j + 1))
                        scalar.activation(wdum[:, :], wstage[0][:, :], AF.Abs,
                                          accum_out=absacc[:, j:j + 1]
                                          ).then_inc(sem["vWS"], 1)
                        if j + 1 < 8:
                            scalar.wait_ge(sem["vWS"], j + 1)
                            scalar.dma_start(wstage[0][:, :],
                                             wffn_d[(j + 1) * 128:(j + 2) * 128, :]
                                             ).then_inc(sem["dWF1"], 16)

                    seed_tile(0)
                    for c in range(4):
                        xn1_chunk(0, c)
                    scalar.dma_start(wstage[0][:, :], wffn_d[0:128, :]
                                     ).then_inc(sem["dWF1"], 16)
                    seed_tile(1)
                    skphi_tile(0, stats_step, 0)
                    for c in range(4):
                        xn1_chunk(1, c)
                    seed_tile(2)
                    skphi_tile(1, stats_step, 4)
                    for c in range(4):
                        xn1_chunk(2, c)
                    seed_tile(3)
                    skphi_tile(2)
                    for c in range(4):
                        xn1_chunk(3, c)
                    skphi_tile(3)
                    for i in range(NTI):
                        sqphi_tile(i)

                @blk.tensor
                def _(tensor):
                    tensor.wait_ge(sem["vIN"], 1)
                    tensor.wait_ge(sem["gIN"], 2)

                    def tr_chunk(g):
                        tensor.wait_ge(sem["sSX"], g + 1)
                        for j in range(2):
                            q = 2 * g + j
                            if q > 1:
                                tensor.wait_ge(sem["vTE"], q - 1)
                            for k4 in range(4):
                                k = 4 * j + k4
                                tt = tensor.transpose(
                                    ptr[j][:, k4 * 128:(k4 + 1) * 128],
                                    xn1_tm[g % 2][:, k * 128:(k + 1) * 128],
                                    id_b[:, :],
                                )
                            tt.then_inc(sem["sTP"], 1)

                    tr_chunk(0)
                    tr_chunk(1)
                    for g in range(NCH):
                        tensor.wait_ge(sem["vTE"], 2 * g + 2)
                        if g > 0:
                            tensor.wait_ge(sem["sSK"], 2 * g)
                        else:
                            tensor.wait_ge(sem["dWg"], 144)
                        for s_ in range(2):
                            tensor.matmul(kps[:, s_ * 512:(s_ + 1) * 512],
                                          ones_row[0:1, :],
                                          bkv_row[0:1, s_ * 512:(s_ + 1) * 512],
                                          start=True, stop=False)
                        for k in range(8):
                            for s_ in range(2):
                                mm = tensor.matmul(
                                    kps[:, s_ * 512:(s_ + 1) * 512],
                                    xn1T_f[:, k, g * 128:(g + 1) * 128],
                                    wq_sb[:, k, E + s_ * 512:E + (s_ + 1) * 512],
                                    start=False, stop=(k == 7),
                                )
                        mm.then_inc(sem["sTK"], 1)
                        if g > 0:
                            tensor.wait_ge(sem["sVV"], g)
                        for s_ in range(2):
                            tensor.matmul(vps[:, s_ * 512:(s_ + 1) * 512],
                                          ones_row[0:1, :],
                                          bkv_row[0:1, E + s_ * 512:E + (s_ + 1) * 512],
                                          start=True, stop=False)
                        for k in range(8):
                            for s_ in range(2):
                                mm = tensor.matmul(
                                    vps[:, s_ * 512:(s_ + 1) * 512],
                                    xn1T_f[:, k, g * 128:(g + 1) * 128],
                                    wq_sb[:, k, 2 * E + s_ * 512:2 * E + (s_ + 1) * 512],
                                    start=False, stop=(k == 7),
                                )
                        mm.then_inc(sem["sTV"], 1)
                        if 2 <= g + 1 < NCH:
                            tr_chunk(g + 1)
                        tensor.wait_ge(sem["sVK"], g + 1)
                        tensor.wait_ge(sem["sVV"], g + 1)
                        for h in range(16):
                            half, slot = h % 2, h // 2
                            tensor.matmul(
                                kvps[64 * half:64 * half + 64, 64 * slot:64 * slot + 64],
                                pk_tm[:, h * 64:(h + 1) * 64],
                                v_tm[:, h * 64:(h + 1) * 64],
                                start=(g == 0 and slot == 0),
                                stop=(g == 15 and slot == 7),
                                tile_position=(0, 64) if half else None,
                                skip_group_check=True,
                            )
                        for p in range(8):
                            mm = tensor.matmul(
                                ksps[:, p:p + 1],
                                pk_tm[:, p * 128:(p + 1) * 128],
                                ones_cb[:, 0:1],
                                start=(g == 0 and p == 0),
                                stop=(g == 15 and p == 7),
                            )
                        mm.then_inc(sem["sTM2"], 1)
                    tensor.wait_ge(sem["dWgQ"], 128)
                    for i in range(NTI):
                        for m in range(8):
                            gm = 8 * i + m
                            if gm == 0:
                                tensor.wait_ge(sem["sSK"], 32)
                            if gm >= 2:
                                tensor.wait_ge(sem["sSQ"], 2 * (gm - 2) + 2)
                            for k in range(8):
                                mm = tensor.matmul(
                                    qps[m % 2][:, :],
                                    wq_sb[:, k, m * 128:(m + 1) * 128],
                                    xn1T_f[:, k, i * 512:(i + 1) * 512],
                                    start=(k == 0), stop=(k == 7),
                                )
                            mm.then_inc(sem["sTQ"], 1)

        with ExitStack() as PB:
            eb = PB.enter_context
            wqT = eb(nc.sbuf_tensor("wqT", [128, 8, E], BF16))
            attnT = eb(nc.sbuf_tensor("attnT", [128, 2, 512], BF16))
            _sg = eb(nc.sbuf_tensor("sg1", [128, E], BF16))
            sg = [wdum, _sg]
            wnum = eb(nc.sbuf_tensor("wnum", [128, E], BF16))
            wqc = [eb(nc.sbuf_tensor(f"wqc{j}", [128, E], BF16)) for j in range(2)]
            wstg = [wstage[0], wstage[0]]
            attn_tm = [eb(nc.sbuf_tensor(f"attn_tm{j}", [128, E], BF16)) for j in range(2)]
            den_sb = eb(nc.sbuf_tensor("den_sb", [128, 1024], F32))
            xn2_sb = eb(nc.sbuf_tensor("xn2_sb", [128, NCH, E], F32))
            gamb = eb(nc.sbuf_tensor("gamb", [128, E], F32))
            betb = eb(nc.sbuf_tensor("betb", [128, E], F32))
            xq_tm = attn_tm
            den_ps = [eb(nc.psum_tensor(f"den_ps{j}", [128, 512], F32)) for j in range(2)]
            attn_ps = [eb(nc.psum_tensor(f"attn_ps{j}", [128, 512], F32)) for j in range(2)]
            dtr_ps = eb(nc.psum_tensor("dtr_ps", [128, 256], F32))
            m4s = [eb(nc.psum_tensor(f"m4s{j}", [128, 512], F32)) for j in range(3)]

            with nc.Block() as blk:

                @blk.sync
                def _(sync):
                    sync.sem_inc(sem["bar1"], 1)
                    sync.wait_ge(sem["bar1"], 5)
                    sync.dma_start(gamb[:, :],
                                   bass.AP(gam_d, 0, [[0, 128], [1, E]])).then_inc(sem["dGB"], 16)
                    sync.dma_start(betb[:, :],
                                   bass.AP(bet_d, 0, [[0, 128], [1, E]])).then_inc(sem["dGB"], 16)
                    for g in range(NCH):
                        i = g // 4
                        sync.wait_ge(sem[f"dAT{i}"], 128)
                        if g >= 2:
                            sync.wait_ge(sem["vSC"], 16 * (g - 1))
                        sync.dma_start(
                            attn_tm[g % 2][:, :],
                            bass.AP(attn_dram, g * 128, [[T, E], [1, 128]]),
                            transpose=True,
                        ).then_inc(sem[f"dATT{g % 2}"], 16)
                    sync.wait_ge(sem["vMX"], 4)
                    sync.dma_start(cc_mx_i[0:1, 0:1], mxs[0:1, 0:1]).then_inc(sem["dMXI"], 16)
                    sync.wait_ge(sem["sCC2"], 1)
                    sync.dma_start(gmax_b[:, :],
                                   bass.AP(cc_mx_o, 0, [[0, 128], [1, 1]])
                                   ).then_inc(sem["dGM"], 16)
                    sync.wait_ge(sem["vAL"], 4)
                    sync.dma_start(scr[0:1, 1:2], alpha[0:1, 0:1]).then_inc(sem["dAB"], 16)
                    sync.wait_ge(sem["dAB"], 16)
                    sync.dma_start(alpha_b[:, :],
                                   bass.AP(scr, 1, [[0, 128], [1, 1]])
                                   ).then_inc(sem["dAB"], 16)
                    sync.wait_ge(sem["dWQ"], 128)
                    sync.wait_ge(sem["sACC"], 16)
                    for k in range(8):
                        sync.dma_start(
                            wqT[:, k, :],
                            bass.AP(wq_dram, k * 128, [[E, E], [1, 128]]),
                            transpose=True,
                        ).then_inc(sem["dWQT"], 16)
                    for g in range(NCH):
                        i = g // 4
                        sync.wait_ge(sem["sRD"], 2 * (g + 1))
                        sync.dma_start(xq_dram[g * 128:(g + 1) * 128, :],
                                       xq_tm[g % 2][:, :]).then_inc(sem[f"dXQ{g % 2}"], 16)
                        if g % 4 == 3:
                            sync.wait_ge(sem["dXQ0"], 16 * (g // 2 + 1))
                            sync.wait_ge(sem["dXQ1"], 16 * (g // 2 + 1))
                            for k in range(8):
                                sync.dma_start(
                                    actT[:, k, i * 512:(i + 1) * 512],
                                    bass.AP(xq_dram, i * 512 * E + k * 128,
                                            [[E, 512], [1, 128]]),
                                    transpose=True,
                                ).then_inc(sem[f"dXQT{i}"], 16)
                    for go in range(NCH):
                        sync.wait_ge(sem["vEV"], 2 * (go + 1))
                        sync.dma_start(out_d[go * 128:(go + 1) * 128, :],
                                       r_sb[:, go, :]).then_inc(sem["dOUT"], 16)
                    sync.wait_ge(sem["dOUT"], 16 * NCH)
                    if debug:
                        sync.dma_start(dbg_kvloc.ap(), kv_full[:, :]).then_inc(sem["dOUT"], 16)
                        sync.dma_start(dbg_z.ap(), z_all[:, :, :]).then_inc(sem["dOUT"], 16)
                        sync.dma_start(dbg_sc[0:1, 0:1], gmax_b[0:1, :]).then_inc(sem["dOUT"], 16)
                        sync.dma_start(dbg_sc[0:1, 1:2], alpha[0:1, :]).then_inc(sem["dOUT"], 16)
                        sync.dma_start(dbg_sc[0:1, 2:4], scb[0:1, :]).then_inc(sem["dOUT"], 16)
                        sync.dma_start(dbg_sc[0:1, 4:5], mxs[0:1, :]).then_inc(sem["dOUT"], 16)
                        sync.dma_start(dbg_sc[0:1, 5:6], s1[0:1, :]).then_inc(sem["dOUT"], 16)
                        sync.wait_ge(sem["dOUT"], 16 * NCH + 7 * 16)

                @blk.gpsimd
                def _(gpsimd):
                    gpsimd.sem_inc(sem["bar1"], 1)
                    gpsimd.wait_ge(sem["bar1"], 5)
                    gpsimd.dma_start(wstg[0][:, :], wffn_d[0:128, :]
                                     ).then_inc(sem["dMK0"], 16)

                    def mask_iter(j):
                        gpsimd.wait_ge(sem["sSGN"], 2 * (j + 1))
                        if j >= 2:
                            gpsimd.wait_ge(sem["sACC"], 2 * (j - 1))
                            gpsimd.wait_ge(sem["dWQ"], 16 * j)
                        gpsimd.tensor_add(wqc[j % 2][:, :], sg[0][:, :],
                                          sg[1][:, :]).then_inc(sem["gMK"], 1)
                        gpsimd.wait_ge(sem["gMK"], 2 * j + 1)
                        gpsimd.dma_start(wq_dram[j * 128:(j + 1) * 128, :],
                                         wqc[j % 2][:, :]).then_inc(sem["dWQ"], 16)
                        if j >= 1:
                            gpsimd.wait_ge(sem["sACC"], 2 * j - 1)
                        gpsimd.tensor_mul(wnum[:, :], wstg[0][:, :],
                                          wqc[j % 2][:, :]).then_inc(sem["gMK"], 1)
                        if j + 1 < 8:
                            gpsimd.wait_ge(sem["gMK"], 2 * j + 2)
                            gpsimd.dma_start(wstg[0][:, :],
                                             wffn_d[(j + 1) * 128:(j + 2) * 128, :]
                                             ).then_inc(sem["dMK0"], 16)

                    for i in range(NTI):
                        mask_iter(2 * i)
                        mask_iter(2 * i + 1)
                        for c in range(4):
                            g = 4 * i + c
                            gpsimd.wait_ge(sem["sY2"], g + 1)
                            gpsimd.wait_ge(sem["dGB"], 32)
                            gpsimd.tensor_mul(xn2_sb[:, g, :], xn2_sb[:, g, :],
                                              gamb[:, :]).then_inc(sem["gGM"], 1)
                            gpsimd.wait_ge(sem["gGM"], 2 * g + 1)
                            gpsimd.tensor_add(xn2_sb[:, g, :], xn2_sb[:, g, :],
                                              betb[:, :]).then_inc(sem["gGM"], 1)
                    gpsimd.wait_ge(sem["vMX"], 2)
                    gpsimd.tensor_reduce(mxs[0:1, 0:1], mxcol[:, 0:1], AX.XYZWC,
                                         ALU.max).then_inc(sem["gMX"], 1)
                    gpsimd.wait_ge(sem["dMXI"], 16)
                    gpsimd.collective_compute(
                        "AllReduce", ALU.max,
                        replica_groups=[[0, 1, 2, 3, 4, 5, 6, 7]],
                        ins=[cc_mx_i.ap().opt()],
                        outs=[cc_mx_o.ap().opt()],
                    ).then_inc(sem["sCC2"], 1)
                    gpsimd.wait_ge(sem["sACC"], 16)
                    gpsimd.tensor_reduce(dnum[0:1, 0:1], numacc[:, :], AX.XYZWC,
                                         ALU.add).then_inc(sem["gRD2"], 1)
                    gpsimd.tensor_reduce(dden[0:1, 0:1], denacc[:, :], AX.XYZWC,
                                         ALU.add).then_inc(sem["gRD2"], 1)

                @blk.vector
                def _(vector):
                    vector.wait_ge(sem["sVQ"], 32)
                    vector.sem_inc(sem["bar1"], 1)
                    vector.wait_ge(sem["bar1"], 5)
                    vb = 0
                    vector.memset(bd[:, :, :], 0.0).then_inc(sem["vBD"], 1)
                    vector.memset(ksbd[:, :], 0.0).then_inc(sem["vBD"], 1)
                    vb += 2
                    vector.wait_ge(sem["gKV"], 1)
                    vector.wait_ge(sem["vBD"], vb)
                    vector.tensor_copy(
                        bass.AP(bd, 0, [[1024, 64], [128, 8], [1, 64]]),
                        bass.AP(kv_full, 0, [[520, 64], [64, 8], [1, 64]]),
                    ).then_inc(sem["vBD"], 1)
                    vector.tensor_copy(
                        bass.AP(bd, 64 * 1024 + 64, [[1024, 64], [128, 8], [1, 64]]),
                        bass.AP(kv_full, 64 * 520, [[520, 64], [64, 8], [1, 64]]),
                    ).then_inc(sem["vBD"], 1)
                    vector.tensor_copy(
                        bass.AP(ksbd, 0, [[16, 64], [2, 8]]),
                        bass.AP(kv_full, 512, [[520, 64], [1, 8]]),
                    ).then_inc(sem["vBD"], 1)
                    vector.tensor_copy(
                        bass.AP(ksbd, 64 * 16 + 1, [[16, 64], [2, 8]]),
                        bass.AP(kv_full, 64 * 520 + 512, [[520, 64], [1, 8]]),
                    ).then_inc(sem["vBD"], 1)
                    vb += 4
                    vz = 0
                    vsc = 0
                    cv = 0
                    nw2 = 0

                    def denz_tile(i):
                        nonlocal vz
                        vector.wait_ge(sem["sTD"], 8 * (i + 1))
                        for p in range(4):
                            vector.tensor_copy(
                                den_sb[32 * p:32 * p + 2, 0:512],
                                den_ps[0][32 * p:32 * p + 2, :]).then_inc(sem["vDE"], 1)
                        for c in range(4):
                            g = 4 * i + c
                            vector.wait_ge(sem["sTT"], 2 * g + 1)
                            vector.tensor_copy(
                                z_all[:, g, 0:8],
                                bass.AP(dtr_ps, 0, [[256, 128], [32, 4], [1, 2]]),
                            ).then_inc(sem["vZ"], 1)
                            vector.wait_ge(sem["sTT"], 2 * g + 2)
                            vector.tensor_copy(
                                z_all[:, g, 8:16],
                                bass.AP(dtr_ps, 128, [[256, 128], [32, 4], [1, 2]]),
                            ).then_inc(sem["vZ"], 1)
                            vz += 2
                            vector.wait_ge(sem["vZ"], vz)
                            vector.tensor_scalar_add(z_all[:, g, :], z_all[:, g, :],
                                                     EPSZ).then_inc(sem["vZ"], 1)
                            vz += 1
                            vector.wait_ge(sem["vZ"], vz)
                            vector.reciprocal(z_all[:, g, :], z_all[:, g, :]).then_inc(sem["vZ"], 1)
                            vz += 1

                    for g in range(NCH):
                        i = g // 4
                        if g % 4 == 0:
                            denz_tile(i)
                        vector.wait_ge(sem[f"dATT{g % 2}"], 16 * (g // 2 + 1))
                        if g > 0:
                            vector.wait_ge(sem["vSC"], vsc)
                        for h in range(16):
                            vector.scalar_tensor_tensor(
                                r_sb[:, g, h * 64:(h + 1) * 64],
                                attn_tm[g % 2][:, h * 64:(h + 1) * 64],
                                z_all[:, g, h:h + 1],
                                r_sb[:, g, h * 64:(h + 1) * 64],
                                ALU.mult, ALU.add,
                            ).then_inc(sem["vSC"], 1)
                        vsc += 16
                        vector.wait_ge(sem["vSC"], vsc)
                        vector.bn_stats(st12[:, 0:6], r_sb[:, g, 0:512]).then_inc(sem["vL2"], 1)
                        vector.bn_stats(st12[:, 6:12], r_sb[:, g, 512:1024]).then_inc(sem["vL2"], 1)
                        cv += 2
                        vector.wait_ge(sem["vL2"], cv)
                        vector.bn_aggr(mvp[:, :], st12[:, :]).then_inc(sem["vL2"], 1)
                        cv += 1
                        vector.wait_ge(sem["vL2"], cv)
                        vector.tensor_copy(mu_all[:, g:g + 1], mvp[:, 0:1]).then_inc(sem["vL2"], 1)
                        vector.tensor_copy(var_all[:, g:g + 1], mvp[:, 1:2]).then_inc(sem["vL2"], 1)
                        cv += 2
                        if g >= 4:
                            gm = g - 4
                            vector.wait_ge(sem["gGM"], 2 * gm + 2)
                            vector.tensor_reduce(mxcols[:, gm:gm + 1], xn2_sb[:, gm, :],
                                                 AX.X, ALU.max, apply_absolute_value=True
                                                 ).then_inc(sem["vBB"], 1)
                        if g % 4 == 3:
                            vector.wait_ge(sem["vL2"], cv)
                            vector.tensor_scalar_add(ve4[:, :], var_all[:, 4 * i:4 * i + 4],
                                                     EPS).then_inc(sem["vNW2"], 1)
                            vector.wait_ge(sem["sL2"], 4 * i + 2)
                            vector.wait_ge(sem["vNW2"], nw2 + 1)
                            vector.tensor_mul(t4[:, :], y04[:, :], y04[:, :]).then_inc(sem["vNW2"], 1)
                            vector.wait_ge(sem["vNW2"], nw2 + 2)
                            vector.tensor_mul(t4[:, :], t4[:, :], ve4[:, :]).then_inc(sem["vNW2"], 1)
                            vector.wait_ge(sem["vNW2"], nw2 + 3)
                            vector.tensor_scalar(t4[:, :], t4[:, :], -0.5, 1.5,
                                                 ALU.mult, ALU.add).then_inc(sem["vNW2"], 1)
                            vector.wait_ge(sem["vNW2"], nw2 + 4)
                            vector.tensor_mul(t4[:, :], y04[:, :], t4[:, :]).then_inc(sem["vNW2"], 1)
                            vector.wait_ge(sem["vNW2"], nw2 + 5)
                            vector.tensor_mul(var2[:, :], t4[:, :], t4[:, :]).then_inc(sem["vNW2"], 1)
                            vector.wait_ge(sem["vNW2"], nw2 + 6)
                            vector.tensor_mul(var2[:, :], var2[:, :],
                                              var_all[:, 4 * i:4 * i + 4]).then_inc(sem["vNW2"], 1)
                            vector.wait_ge(sem["vNW2"], nw2 + 7)
                            vector.tensor_scalar_add(var2[:, :], var2[:, :],
                                                     EPS).then_inc(sem["vNW2"], 1)
                            nw2 += 8
                            vector.wait_ge(sem["sL2"], 4 * i + 4)
                            vector.tensor_mul(t4b[:, :], y04b[:, :], y04b[:, :]).then_inc(sem["vNW2"], 1)
                            vector.wait_ge(sem["vNW2"], nw2 + 1)
                            vector.tensor_mul(t4b[:, :], t4b[:, :], var2[:, :]).then_inc(sem["vNW2"], 1)
                            vector.wait_ge(sem["vNW2"], nw2 + 2)
                            vector.tensor_scalar(t4b[:, :], t4b[:, :], -0.5, 1.5,
                                                 ALU.mult, ALU.add).then_inc(sem["vNW2"], 1)
                            vector.wait_ge(sem["vNW2"], nw2 + 3)
                            vector.tensor_mul(t4b[:, :], y04b[:, :], t4b[:, :]).then_inc(sem["vNW2"], 1)
                            vector.wait_ge(sem["vNW2"], nw2 + 4)
                            vector.tensor_mul(s2_all[:, 4 * i:4 * i + 4], t4[:, :],
                                              t4b[:, :]).then_inc(sem["vNW2"], 1)
                            vector.wait_ge(sem["vNW2"], nw2 + 5)
                            vector.tensor_mul(nm_all[:, 4 * i:4 * i + 4],
                                              mu_all[:, 4 * i:4 * i + 4],
                                              s2_all[:, 4 * i:4 * i + 4]).then_inc(sem["vNW2"], 1)
                            vector.wait_ge(sem["vNW2"], nw2 + 6)
                            vector.tensor_scalar_mul(nm_all[:, 4 * i:4 * i + 4],
                                                     nm_all[:, 4 * i:4 * i + 4],
                                                     -1.0).then_inc(sem["vNW2"], 1)
                            nw2 += 7
                    for gm in range(NCH - 4, NCH):
                        vector.wait_ge(sem["gGM"], 2 * gm + 2)
                        vector.tensor_reduce(mxcols[:, gm:gm + 1], xn2_sb[:, gm, :],
                                             AX.X, ALU.max, apply_absolute_value=True
                                             ).then_inc(sem["vBB"], 1)
                    vector.wait_ge(sem["vBB"], NCH)
                    vector.tensor_reduce(mxcol[:, 0:1], mxcols[:, :], AX.X,
                                         ALU.max).then_inc(sem["vMX"], 1)
                    vector.sem_inc(sem["vMX"], 1)
                    vector.wait_ge(sem["gMX"], 1)
                    vector.sem_inc(sem["vMX"], 2)
                    vector.wait_ge(sem["gRD2"], 2)
                    vector.tensor_scalar_max(dden[0:1, 0:1], dden[0:1, 0:1], 1.0
                                             ).then_inc(sem["vAL"], 1)
                    vector.wait_ge(sem["vAL"], 1)
                    vector.reciprocal(dden[0:1, 0:1], dden[0:1, 0:1]).then_inc(sem["vAL"], 1)
                    vector.wait_ge(sem["vAL"], 2)
                    vector.tensor_mul(alpha[0:1, 0:1], dnum[0:1, 0:1],
                                      dden[0:1, 0:1]).then_inc(sem["vAL"], 1)
                    vector.wait_ge(sem["vAL"], 3)
                    vector.tensor_scalar_mul(alpha[0:1, 0:1], alpha[0:1, 0:1],
                                             0.5).then_inc(sem["vAL"], 1)
                    vector.wait_ge(sem["dGM"], 16)
                    vector.tensor_scalar_max(t_b[:, :], gmax_b[:, :], 1e-8
                                             ).then_inc(sem["vSCL"], 1)
                    vector.wait_ge(sem["vSCL"], 1)
                    vector.tensor_scalar_mul(t_b[:, :], t_b[:, :],
                                             1.0 / QMAX).then_inc(sem["vSCL"], 1)
                    vector.wait_ge(sem["vSCL"], 2)
                    vector.reciprocal(scb[:, 0:1], t_b[:, :]).then_inc(sem["vSCL"], 1)
                    vector.wait_ge(sem["dAB"], 32)
                    vector.tensor_mul(scb[:, 1:2], t_b[:, :],
                                      alpha_b[:, :]).then_inc(sem["vSCL"], 1)
                    vector.wait_ge(sem["vSCL"], 4)
                    for ge in range(NCH):
                        for h in range(2):
                            q = 2 * ge + h
                            vector.wait_ge(sem["sTM4"], q + 1)
                            vector.scalar_tensor_tensor(
                                r_sb[:, ge, h * 512:(h + 1) * 512],
                                m4s[q % 3][:, :], scb[:, 1:2],
                                r_sb[:, ge, h * 512:(h + 1) * 512],
                                ALU.mult, ALU.add,
                            ).then_inc(sem["vEV"], 1)

                @blk.scalar
                def _(scalar):
                    scalar.wait_ge(sem["sSQ"], 64)
                    scalar.wait_ge(sem["sSX"], 16)
                    scalar.sem_inc(sem["bar1"], 1)
                    scalar.wait_ge(sem["bar1"], 5)

                    def sign_pair(j):
                        scalar.wait_ge(sem["dMK0"], 16 * (j + 1))
                        scalar.wait_ge(sem["vDL"], 2)
                        if j >= 1:
                            scalar.wait_ge(sem["gMK"], 2 * j - 1)
                        scalar.activation(sg[0][:, :], wstg[0][:, :], AF.Sign,
                                          bias=ndelta_c[:, 0:1]).then_inc(sem["sSGN"], 1)
                        scalar.activation(sg[1][:, :], wstg[0][:, :], AF.Sign,
                                          bias=delta_c[:, 0:1]).then_inc(sem["sSGN"], 1)

                    def accum_pair(j):
                        scalar.wait_ge(sem["gMK"], 2 * j + 2)
                        if j > 0:
                            scalar.wait_ge(sem["sACC"], 2 * j)
                        scalar.activation(wqT[:, 0, :], wnum[:, :], AF.Identity,
                                          accum_out=numacc[:, j:j + 1]
                                          ).then_inc(sem["sACC"], 1)
                        scalar.wait_ge(sem["sACC"], 2 * j + 1)
                        scalar.activation(wqT[:, 0, :], wqc[j % 2][:, :], AF.Abs,
                                          accum_out=denacc[:, j:j + 1]
                                          ).then_inc(sem["sACC"], 1)

                    sign_pair(0)
                    sign_pair(1)
                    for i in range(NTI):
                        scalar.wait_ge(sem["sTD"], 8 * (i + 1))
                        for p in range(4):
                            scalar.copy(den_sb[32 * p:32 * p + 2, 512:1024],
                                        den_ps[1][32 * p:32 * p + 2, :]).then_inc(sem["sDE"], 1)
                        for p in range(8):
                            if p >= 2:
                                scalar.wait_ge(sem[f"dAT{i}"], 16 * p)
                            elif i > 0:
                                scalar.wait_ge(sem[f"dAT{i - 1}"], 128)
                            scalar.wait_ge(sem["sTM3"], 8 * i + p + 1)
                            scalar.copy(attnT[:, p % 2, :],
                                        attn_ps[p % 2][:, :]).then_inc(sem["sSAT"], 1)
                            scalar.wait_ge(sem["sSAT"], 8 * i + p + 1)
                            scalar.dma_start(
                                bass.AP(attn_dram, (p * 128) * T + i * 512,
                                        [[T, 128], [1, 512]]),
                                attnT[:, p % 2, :],
                            ).then_inc(sem[f"dAT{i}"], 16)
                        scalar.wait_ge(sem["vL2"], 5 * (4 * i + 4))
                        if i > 0:
                            scalar.wait_ge(sem["vNW2"], 15 * i)
                        scalar.activation(y04[:, :], var_all[:, 4 * i:4 * i + 4],
                                          AF.Ln, bias=eps_c[:, 0:1]).then_inc(sem["sL2"], 1)
                        scalar.wait_ge(sem["sL2"], 4 * i + 1)
                        scalar.activation(y04[:, :], y04[:, :], AF.Exp,
                                          scale=-0.5).then_inc(sem["sL2"], 1)
                        scalar.wait_ge(sem["vNW2"], 15 * i + 8)
                        scalar.activation(y04b[:, :], var2[:, :], AF.Ln).then_inc(sem["sL2"], 1)
                        scalar.wait_ge(sem["sL2"], 4 * i + 3)
                        scalar.activation(y04b[:, :], y04b[:, :], AF.Exp,
                                          scale=-0.5).then_inc(sem["sL2"], 1)
                        for c in range(4):
                            g = 4 * i + c
                            scalar.wait_ge(sem["vNW2"], 15 * (i + 1))
                            scalar.activation(xn2_sb[:, g, :], r_sb[:, g, :], AF.Identity,
                                              bias=nm_all[:, g:g + 1],
                                              scale=s2_all[:, g:g + 1]).then_inc(sem["sY2"], 1)
                        accum_pair(2 * i)
                        accum_pair(2 * i + 1)
                        if i < 3:
                            sign_pair(2 * i + 2)
                            sign_pair(2 * i + 3)
                    for g in range(NCH):
                        scalar.wait_ge(sem["vSCL"], 3)
                        scalar.wait_ge(sem["vBB"], g + 1)
                        if g > 0:
                            scalar.wait_ge(sem["sRD"], 2 * g)
                        scalar.activation(den_sb[:, :], xn2_sb[:, g, :], AF.Identity,
                                          bias=rc_c[:, 0:1],
                                          scale=scb[:, 0:1]).then_inc(sem["sRD"], 1)
                        scalar.wait_ge(sem["sRD"], 2 * g + 1)
                        if g >= 2:
                            scalar.wait_ge(sem[f"dXQ{g % 2}"], 16 * (g // 2))
                        scalar.activation(xq_tm[g % 2][:, :], den_sb[:, :], AF.Identity,
                                          bias=nrc_c[:, 0:1]).then_inc(sem["sRD"], 1)

                @blk.tensor
                def _(tensor):
                    tensor.wait_ge(sem["sTQ"], 32)
                    tensor.sem_inc(sem["bar1"], 1)
                    tensor.wait_ge(sem["bar1"], 5)
                    tensor.wait_ge(sem["vBD"], 6)
                    for i in range(NTI):
                        if i > 0:
                            tensor.wait_ge(sem["vDE"], 4 * i)
                            tensor.wait_ge(sem["sDE"], 4 * i)
                        for p in range(8):
                            bidx, prow = p // 4, p % 4
                            tensor.matmul(
                                den_ps[bidx][32 * prow:32 * prow + 2, :],
                                ksbd[:, 2 * p:2 * p + 2],
                                actT[:, p, i * 512:(i + 1) * 512],
                                start=True, stop=True,
                                tile_position=(0, 32 * prow),
                            ).then_inc(sem["sTD"], 1)
                        for c in range(4):
                            g = 4 * i + c
                            tensor.wait_ge(sem["vDE"], 4 * (i + 1))
                            tensor.wait_ge(sem["sDE"], 4 * (i + 1))
                            if g > 0:
                                tensor.wait_ge(sem["vZ"], 4 * g - 2)
                            tensor.transpose(dtr_ps[:, 0:128],
                                             den_sb[:, c * 128:(c + 1) * 128],
                                             id_f32[:, :]).then_inc(sem["sTT"], 1)
                            tensor.wait_ge(sem["vZ"], 4 * g + 1)
                            tensor.transpose(dtr_ps[:, 128:256],
                                             den_sb[:, 512 + c * 128:512 + (c + 1) * 128],
                                             id_f32[:, :]).then_inc(sem["sTT"], 1)
                        for p in range(8):
                            np_ = 8 * i + p
                            if np_ >= 2:
                                tensor.wait_ge(sem["sSAT"], np_ - 1)
                            tensor.matmul(
                                attn_ps[p % 2][:, :],
                                bd[:, p, :],
                                actT[:, p, i * 512:(i + 1) * 512],
                                start=True, stop=True,
                            ).then_inc(sem["sTM3"], 1)
                    tensor.wait_ge(sem["dWQT"], 128)
                    for g in range(NCH):
                        i = g // 4
                        tensor.wait_ge(sem[f"dXQT{i}"], 128)
                        for s_ in range(2):
                            q = 2 * g + s_
                            if q >= 3:
                                tensor.wait_ge(sem["vEV"], q - 2)
                            for k in range(8):
                                mm = tensor.matmul(
                                    m4s[q % 3][:, :],
                                    actT[:, k, g * 128:(g + 1) * 128],
                                    wqT[:, k, s_ * 512:(s_ + 1) * 512],
                                    start=(k == 0), stop=(k == 7),
                                )
                            mm.then_inc(sem["sTM4"], 1)

    return nc


def kernel(L0_attn_x, L0_attn_W_qkv, L0_attn_b_qkv, L0_attn_W_proj,
           L0_attn_b_proj, L0_ffn_weight, L0_ffn_gamma, L0_ffn_beta):
    x = np.ascontiguousarray(np.asarray(L0_attn_x, dtype=np.float32)).reshape(B * S, E)
    wqkv = np.ascontiguousarray(np.asarray(L0_attn_W_qkv, dtype=np.float32))
    bqkv = np.ascontiguousarray(np.asarray(L0_attn_b_qkv, dtype=np.float32))
    wffn = np.ascontiguousarray(np.asarray(L0_ffn_weight, dtype=np.float32))
    gam = np.ascontiguousarray(np.asarray(L0_ffn_gamma, dtype=np.float32))
    bet = np.ascontiguousarray(np.asarray(L0_ffn_beta, dtype=np.float32))
    nc = build_nc()
    in_maps = [
        {"x": np.ascontiguousarray(x[c * T:(c + 1) * T]),
         "wqkv": wqkv, "bqkv": bqkv, "wffn": wffn, "gamma": gam, "beta": bet}
        for c in range(NC_)
    ]
    res = run_bass_kernel_spmd(nc, in_maps, core_ids=list(range(NC_)))
    out = np.concatenate([res.results[c]["out"] for c in range(NC_)], axis=0)
    return out.reshape(B, S, E)


if __name__ == "__main__":
    pass


# version 13
# speedup vs baseline: 1.2931x; 1.2931x over previous
import numpy as np
import ml_dtypes
import concourse.bass as bass
import concourse.mybir as mybir
from concourse.bass_utils import run_bass_kernel_spmd
from contextlib import ExitStack

F32 = mybir.dt.float32
BF16 = mybir.dt.bfloat16
AF = mybir.ActivationFunctionType
ALU = mybir.AluOpType
AX = mybir.AxisListType

B, S, E, H, D = 4, 4096, 1024, 16, 64
NC_ = 8
T = 2048
NCH = 16
NTI = 4
EPS = 1e-5
EPSZ = 1e-6
RC = 12582912.0
QMAX = 127.0
THRESH = 0.7


def build_nc(debug=False):
    nc = bass.Bass("TRN2", target_bir_lowering=False,
                   dynamic_dma_scratch_size=8192)

    x_d = nc.dram_tensor("x", [T, E], F32, kind="ExternalInput")
    wqkv_d = nc.dram_tensor("wqkv", [E, 3 * E], F32, kind="ExternalInput")
    bqkv_d = nc.dram_tensor("bqkv", [3 * E], F32, kind="ExternalInput")
    wffn_d = nc.dram_tensor("wffn", [E, E], F32, kind="ExternalInput")
    gam_d = nc.dram_tensor("gamma", [E], F32, kind="ExternalInput")
    bet_d = nc.dram_tensor("beta", [E], F32, kind="ExternalInput")
    out_d = nc.dram_tensor("out", [T, E], F32, kind="ExternalOutput")

    cc_kv_i = nc.dram_tensor("cc_kv_i", [128, 520], F32)
    cc_kv_o = nc.dram_tensor("cc_kv_o", [128, 520], F32)
    cc_mx_i = nc.dram_tensor("cc_mx_i", [1, 1], F32)
    cc_mx_o = nc.dram_tensor("cc_mx_o", [1, 1], F32)
    scr = nc.dram_tensor("scr", [1, 4], F32)
    if debug:
        dbg_kvloc = nc.dram_tensor("dbg_kvloc", [128, 520], F32, kind="ExternalOutput")
        dbg_z = nc.dram_tensor("dbg_z", [128, 256], F32, kind="ExternalOutput")
        dbg_sc = nc.dram_tensor("dbg_sc", [128, 16], F32, kind="ExternalOutput")

    with ExitStack() as P:
        e = P.enter_context
        r_sb = e(nc.sbuf_tensor("r_sb", [128, NCH, E], F32))
        actT = e(nc.sbuf_tensor("actT", [128, 8, T], BF16))
        kv_full = e(nc.sbuf_tensor("kv_full", [128, 520], F32))
        bd = e(nc.sbuf_tensor("bd", [128, 8, 128], BF16))
        ksbd = e(nc.sbuf_tensor("ksbd", [128, 16], BF16))
        z_all = e(nc.sbuf_tensor("z_all", [128, NCH, H], F32))
        mu_all = e(nc.sbuf_tensor("mu_all", [128, NCH], F32))
        var_all = e(nc.sbuf_tensor("var_all", [128, NCH], F32))
        mvp = e(nc.sbuf_tensor("mvp", [128, 2], F32))
        st12 = e(nc.sbuf_tensor("st12", [128, 12], F32))
        rstd_all = e(nc.sbuf_tensor("rstd_all", [128, NCH], F32))
        nm_all = e(nc.sbuf_tensor("nm_all", [128, NCH], F32))
        s2_all = e(nc.sbuf_tensor("s2_all", [128, NCH], F32))
        ve4 = e(nc.sbuf_tensor("ve4", [128, 4], F32))
        y04 = e(nc.sbuf_tensor("y04", [128, 4], F32))
        t4 = e(nc.sbuf_tensor("t4", [128, 4], F32))
        var2 = e(nc.sbuf_tensor("var2", [128, 4], F32))
        y04b = e(nc.sbuf_tensor("y04b", [128, 4], F32))
        t4b = e(nc.sbuf_tensor("t4b", [128, 4], F32))
        bq_sb = e(nc.sbuf_tensor("bq_sb", [128, 8], F32))
        ones_row = e(nc.sbuf_tensor("ones_row", [1, 128], BF16))
        ones_cb = e(nc.sbuf_tensor("ones_cb", [128, 1], BF16))
        ones_cf = e(nc.sbuf_tensor("ones_cf", [128, 1], F32))
        eps_c = e(nc.sbuf_tensor("eps_c", [128, 1], F32))
        id_f32 = e(nc.sbuf_tensor("id_f32", [128, 128], F32))
        id_b = e(nc.sbuf_tensor("id_b", [128, 128], BF16))
        absacc = e(nc.sbuf_tensor("absacc", [128, 8], F32))
        numacc = e(nc.sbuf_tensor("numacc", [128, 8], F32))
        denacc = e(nc.sbuf_tensor("denacc", [128, 8], F32))
        s1 = e(nc.sbuf_tensor("s1", [1, 1], F32))
        dnum = e(nc.sbuf_tensor("dnum", [1, 1], F32))
        dden = e(nc.sbuf_tensor("dden", [1, 1], F32))
        alpha = e(nc.sbuf_tensor("alpha", [1, 1], F32))
        delta_c = e(nc.sbuf_tensor("delta_c", [128, 1], F32))
        ndelta_c = e(nc.sbuf_tensor("ndelta_c", [128, 1], F32))
        mxcols = e(nc.sbuf_tensor("mxcols", [128, NCH], F32))
        mxcol = e(nc.sbuf_tensor("mxcol", [128, 1], F32))
        mxs = e(nc.sbuf_tensor("mxs", [1, 1], F32))
        gmax = e(nc.sbuf_tensor("gmax", [1, 1], F32))
        scpair = e(nc.sbuf_tensor("scpair", [1, 2], F32))
        scb = e(nc.sbuf_tensor("scb", [128, 2], F32))
        t_b = e(nc.sbuf_tensor("t_b", [128, 1], F32))
        rc_c = e(nc.sbuf_tensor("rc_c", [128, 1], F32))
        nrc_c = e(nc.sbuf_tensor("nrc_c", [128, 1], F32))
        alpha_b = e(nc.sbuf_tensor("alpha_b", [128, 1], F32))
        gmax_b = e(nc.sbuf_tensor("gmax_b", [128, 1], F32))

        _ws = e(nc.sbuf_tensor("wstage0", [128, E], F32))
        wstage = [_ws, _ws]
        wdum = e(nc.sbuf_tensor("wdum", [128, E], BF16))

        names = ["dX0", "dX1", "dX2", "dX3", "dWs", "dWg", "gMK",
                 "dWgQ", "sSGN", "dMK0", "sRD", "dAB", "dGM", "sACC",
                 "sTP", "vTE", "vIN", "gIN",
                 "vST", "sLN", "vNW", "sSX",
                 "sTQ", "sSQ", "sVQ", "sTK", "sTV", "sSK", "sVK", "sVV", "sTM2",
                 "vKL", "dKVI", "sCC", "dKVO", "gKV",
                 "dWF1", "vWS", "gRD", "vDL", "dSCR", "dDEL",
                 "gRD2", "vAL",
                 "bar1", "vBD", "sTB", "vZ", "vSC",
                 "dGB", "vL2", "sL2", "vNW2", "sY2", "gGM", "vBB",
                 "vMX", "gMX", "dMXI", "sCC2", "vSCL",
                 "sTPC", "vTC", "sTM4", "vEV", "dOUT"]
        sem = {n: e(nc.semaphore(n)) for n in names}

        with ExitStack() as PA:
            ea = PA.enter_context
            wq_sb = ea(nc.sbuf_tensor("wq_sb", [128, 8, 3 * E], BF16))
            xn1T_f = ea(nc.sbuf_tensor("xn1T_f", [128, 8, T], BF16))
            xn1_tm = [ea(nc.sbuf_tensor(f"xn1_tm{j}", [128, E], BF16)) for j in range(2)]
            pk_tm = ea(nc.sbuf_tensor("pk_tm", [128, E], BF16))
            v_tm = ea(nc.sbuf_tensor("v_tm", [128, E], BF16))
            ek_b = ea(nc.sbuf_tensor("ek_b", [128, E], BF16))
            rk_b = ea(nc.sbuf_tensor("rk_b", [128, E], BF16))
            eq_b = ea(nc.sbuf_tensor("eq_b", [128, 512], BF16))
            rq_b = ea(nc.sbuf_tensor("rq_b", [128, 512], BF16))
            bkv_row = ea(nc.sbuf_tensor("bkv_row", [1, 2 * E], BF16))
            kv_loc = ea(nc.sbuf_tensor("kv_loc", [128, 520], F32))
            kvps = ea(nc.psum_tensor("kvps", [128, 512], F32))
            ksps = ea(nc.psum_tensor("ksps", [128, 8], F32))
            ptr = [ea(nc.psum_tensor(f"ptr{j}", [128, 512], BF16)) for j in range(2)]
            kps = ea(nc.psum_tensor("kps", [128, 1024], F32))
            vps = ea(nc.psum_tensor("vps", [128, 1024], F32))
            qps = [kps[:, 0:512], kps[:, 512:1024]]

            with nc.Block() as blk:

                @blk.sync
                def _(sync):
                    for i in range(NTI):
                        sync.dma_start(
                            r_sb[:, 4 * i:4 * i + 4, :],
                            x_d.ap().rearrange("(c p) f -> p c f", p=128)[:, 4 * i:4 * i + 4, :],
                        ).then_inc(sem[f"dX{i}"], 16)
                    sync.dma_start(bq_sb[:, :],
                                   bass.AP(bqkv_d, 0, [[1, 128], [128, 8]]),
                                   allow_slow_non_contiguous=True,
                                   ).then_inc(sem["dWs"], 16)
                    for _i in range(NTI):
                        sync.wait_ge(sem[f"dX{_i}"], 16)

                @blk.gpsimd
                def _(gpsimd):
                    gpsimd.memset(id_b[:, :], 1.0).then_inc(sem["gIN"], 1)
                    gpsimd.wait_ge(sem["gIN"], 1)
                    gpsimd.affine_select(id_b[:, :], id_b[:, :], [[-1, 128]],
                                         ALU.is_equal, 0.0,
                                         channel_multiplier=1).then_inc(sem["gIN"], 1)
                    gpsimd.memset(id_f32[:, :], 1.0).then_inc(sem["gIN"], 1)
                    gpsimd.wait_ge(sem["gIN"], 3)
                    gpsimd.affine_select(id_f32[:, :], id_f32[:, :], [[-1, 128]],
                                         ALU.is_equal, 0.0,
                                         channel_multiplier=1).then_inc(sem["gIN"], 1)
                    gpsimd.dma_start(
                        bkv_row[0:1, :],
                        bass.AP(bqkv_d, E, [[2 * E, 1], [1, 2 * E]]),
                    ).then_inc(sem["dWg"], 16)
                    for k in range(8):
                        gpsimd.dma_start(
                            wq_sb[:, k, E:3 * E],
                            wqkv_d[k * 128:(k + 1) * 128, E:3 * E]
                        ).then_inc(sem["dWg"], 16)
                    for k in range(8):
                        gpsimd.dma_start(
                            wq_sb[:, k, 0:E],
                            wqkv_d[k * 128:(k + 1) * 128, 0:E]
                        ).then_inc(sem["dWgQ"], 16)
                    gpsimd.wait_ge(sem["vKL"], 2)
                    gpsimd.dma_start(cc_kv_i[0:128, :], kv_loc[:, :]).then_inc(sem["dKVI"], 16)
                    gpsimd.wait_ge(sem["dKVI"], 16)
                    gpsimd.collective_compute(
                        "AllReduce", ALU.add,
                        replica_groups=[[0, 1], [2, 3], [4, 5], [6, 7]],
                        ins=[cc_kv_i.ap().opt()],
                        outs=[cc_kv_o.ap().opt()],
                    ).then_inc(sem["sCC"], 1)
                    gpsimd.wait_ge(sem["vWS"], 8)
                    gpsimd.tensor_reduce(s1[0:1, 0:1], absacc[:, :], AX.XYZWC,
                                         ALU.add).then_inc(sem["gRD"], 1)
                    gpsimd.wait_ge(sem["gRD"], 1)
                    gpsimd.tensor_scalar_mul(s1[0:1, 0:1], s1[0:1, 0:1],
                                             THRESH / (E * E)).then_inc(sem["vDL"], 1)
                    gpsimd.wait_ge(sem["vDL"], 1)
                    gpsimd.dma_start(scr[0:1, 0:1], s1[0:1, 0:1]).then_inc(sem["dSCR"], 16)
                    gpsimd.wait_ge(sem["dSCR"], 16)
                    gpsimd.dma_start(delta_c[:, :],
                                     bass.AP(scr, 0, [[0, 128], [1, 1]])
                                     ).then_inc(sem["dDEL"], 16)
                    gpsimd.wait_ge(sem["dDEL"], 16)
                    gpsimd.tensor_scalar_mul(ndelta_c[:, :], delta_c[:, :],
                                             -1.0).then_inc(sem["vDL"], 1)
                    gpsimd.wait_ge(sem["sCC"], 1)
                    gpsimd.dma_start(kv_full[:, :], cc_kv_o[0:128, :]).then_inc(sem["dKVO"], 16)
                    gpsimd.wait_ge(sem["dKVO"], 16)
                    gpsimd.sem_inc(sem["gKV"], 1)

                @blk.vector
                def _(vector):
                    cv = 0
                    vector.memset(ones_row[:, :], 1.0)
                    vector.memset(ones_cb[:, :], 1.0)
                    vector.memset(ones_cf[:, :], 1.0)
                    vector.memset(eps_c[:, :], EPS)
                    vector.memset(rc_c[:, :], RC)
                    vector.memset(nrc_c[:, :], -RC)
                    vector.sem_inc(sem["vIN"], 1)
                    nw = 0

                    def ln1_chunk(g):
                        nonlocal cv
                        i = g // 4
                        vector.wait_ge(sem[f"dX{i}"], 16)
                        if g > 0:
                            vector.wait_ge(sem["vST"], cv)
                        vector.bn_stats(st12[:, 0:6], r_sb[:, g, 0:512]).then_inc(sem["vST"], 1)
                        vector.bn_stats(st12[:, 6:12], r_sb[:, g, 512:1024]).then_inc(sem["vST"], 1)
                        cv += 2
                        vector.wait_ge(sem["vST"], cv)
                        vector.bn_aggr(mvp[:, :], st12[:, :]).then_inc(sem["vST"], 1)
                        cv += 1
                        vector.wait_ge(sem["vST"], cv)
                        vector.tensor_copy(mu_all[:, g:g + 1], mvp[:, 0:1]).then_inc(sem["vST"], 1)
                        vector.tensor_copy(var_all[:, g:g + 1], mvp[:, 1:2]).then_inc(sem["vST"], 1)
                        cv += 2
                        assert cv == 5 * (g + 1)

                    def ln1_newton(i):
                        nonlocal cv, nw
                        vector.wait_ge(sem["vST"], cv)
                        vector.tensor_scalar_add(
                            ve4[:, :], var_all[:, 4 * i:4 * i + 4], EPS
                        ).then_inc(sem["vNW"], 1)
                        vector.wait_ge(sem["sLN"], 2 * (i + 1))
                        vector.wait_ge(sem["vNW"], nw + 1)
                        vector.tensor_mul(t4[:, :], y04[:, :], y04[:, :]).then_inc(sem["vNW"], 1)
                        vector.wait_ge(sem["vNW"], nw + 2)
                        vector.tensor_mul(t4[:, :], t4[:, :], ve4[:, :]).then_inc(sem["vNW"], 1)
                        vector.wait_ge(sem["vNW"], nw + 3)
                        vector.tensor_scalar(t4[:, :], t4[:, :], -0.5, 1.5,
                                             ALU.mult, ALU.add).then_inc(sem["vNW"], 1)
                        vector.wait_ge(sem["vNW"], nw + 4)
                        vector.tensor_mul(rstd_all[:, 4 * i:4 * i + 4],
                                          y04[:, :], t4[:, :]).then_inc(sem["vNW"], 1)
                        vector.wait_ge(sem["vNW"], nw + 5)
                        vector.tensor_mul(nm_all[:, 4 * i:4 * i + 4],
                                          mu_all[:, 4 * i:4 * i + 4],
                                          rstd_all[:, 4 * i:4 * i + 4]).then_inc(sem["vNW"], 1)
                        vector.wait_ge(sem["vNW"], nw + 6)
                        vector.tensor_scalar_mul(nm_all[:, 4 * i:4 * i + 4],
                                                 nm_all[:, 4 * i:4 * i + 4],
                                                 -1.0).then_inc(sem["vNW"], 1)
                        nw += 7

                    def ln1_tile(i):
                        for c in range(4):
                            ln1_chunk(4 * i + c)
                        ln1_newton(i)

                    def kphi_tile(i, cb=None, pre=None):
                        for c in range(4):
                            g = 4 * i + c
                            if pre is not None:
                                pre(c)
                            vector.wait_ge(sem["sSK"], 2 * (g + 1))
                            if g > 0:
                                vector.wait_ge(sem["sTM2"], g)
                            vector.scalar_tensor_tensor(
                                pk_tm[:, :], ek_b[:, :], ones_cf[:, 0:1], rk_b[:, :],
                                ALU.min, ALU.add,
                            ).then_inc(sem["sVK"], 1)
                            vector.wait_ge(sem["sTV"], g + 1)
                            vector.tensor_copy(v_tm[:, :], vps[:, :]).then_inc(sem["sVV"], 1)
                            if cb is not None:
                                cb(c)

                    def qphi_tile(i):
                        for m in range(8):
                            gm = 8 * i + m
                            vector.wait_ge(sem["sSQ"], 2 * gm + 2)
                            vector.scalar_tensor_tensor(
                                actT[:, m, i * 512:(i + 1) * 512],
                                eq_b[:, :], ones_cf[:, 0:1], rq_b[:, :],
                                ALU.min, ALU.add,
                            ).then_inc(sem["sVQ"], 1)

                    def kv_evac():
                        vector.wait_ge(sem["sTM2"], 16)
                        vector.tensor_copy(kv_loc[:, 0:512], kvps[:, :]).then_inc(sem["vKL"], 1)
                        vector.tensor_copy(kv_loc[:, 512:520], ksps[:, :]).then_inc(sem["vKL"], 1)

                    def evac_chunk(g):
                        for j in range(2):
                            vector.wait_ge(sem["sTP"], 2 * g + j + 1)
                            vector.tensor_copy(
                                bass.AP(xn1T_f, (4 * j) * T + g * 128,
                                        [[8 * T, 128], [T, 4], [1, 128]]),
                                ptr[j][:, :],
                            ).then_inc(sem["vTE"], 1)

                    ln1_tile(0)
                    evac_chunk(0)
                    ln1_tile(1)
                    evac_chunk(1)
                    for i in range(NTI):
                        def pre(c, i=i):
                            g = 4 * i + c
                            if 2 <= g + 1 < NCH:
                                evac_chunk(g + 1)

                        def cb(c, i=i):
                            if i + 2 < NTI:
                                ln1_chunk(4 * (i + 2) + c)
                                if c == 3:
                                    ln1_newton(i + 2)
                        kphi_tile(i, cb, pre)
                    kv_evac()
                    for i in range(NTI):
                        qphi_tile(i)

                @blk.scalar
                def _(scalar):

                    def seed_tile(i):
                        scalar.wait_ge(sem["vST"], 5 * (4 * i + 4))
                        if i > 0:
                            scalar.wait_ge(sem["vNW"], 7 * i)
                        scalar.wait_ge(sem["vIN"], 1)
                        scalar.activation(y04[:, :], var_all[:, 4 * i:4 * i + 4],
                                          AF.Ln, bias=eps_c[:, 0:1]).then_inc(sem["sLN"], 1)
                        scalar.wait_ge(sem["sLN"], 2 * i + 1)
                        scalar.activation(y04[:, :], y04[:, :], AF.Exp,
                                          scale=-0.5).then_inc(sem["sLN"], 1)

                    def xn1_chunk(i, c):
                        g = 4 * i + c
                        scalar.wait_ge(sem["vNW"], 7 * (i + 1))
                        if g >= 2:
                            scalar.wait_ge(sem["sTP"], 2 * (g - 1))
                        scalar.activation(xn1_tm[g % 2][:, :], r_sb[:, g, :], AF.Identity,
                                          bias=nm_all[:, g:g + 1],
                                          scale=rstd_all[:, g:g + 1]).then_inc(sem["sSX"], 1)

                    def skphi_tile(i, cb=None, cb0=0):
                        for c in range(4):
                            g = 4 * i + c
                            scalar.wait_ge(sem["sTK"], g + 1)
                            if g > 0:
                                scalar.wait_ge(sem["sVK"], g)
                            scalar.activation(ek_b[:, :], kps[:, :], AF.Exp).then_inc(sem["sSK"], 1)
                            scalar.activation(rk_b[:, :], kps[:, :], AF.Relu).then_inc(sem["sSK"], 1)
                            if cb is not None:
                                cb(cb0 + c)

                    def sqphi_tile(i):
                        if i == 0:
                            scalar.wait_ge(sem["dWs"], 16)
                        for m in range(8):
                            gm = 8 * i + m
                            scalar.wait_ge(sem["sTQ"], gm + 1)
                            if gm > 0:
                                scalar.wait_ge(sem["sVQ"], gm)
                            scalar.activation(eq_b[:, :], qps[m % 2][:, :], AF.Exp,
                                              bias=bq_sb[:, m:m + 1]).then_inc(sem["sSQ"], 1)
                            scalar.activation(rq_b[:, :], qps[m % 2][:, :], AF.Relu,
                                              bias=bq_sb[:, m:m + 1]).then_inc(sem["sSQ"], 1)

                    def stats_step(j):
                        scalar.wait_ge(sem["dWF1"], 16 * (j + 1))
                        scalar.activation(wdum[:, :], wstage[0][:, :], AF.Abs,
                                          accum_out=absacc[:, j:j + 1]
                                          ).then_inc(sem["vWS"], 1)
                        if j + 1 < 8:
                            scalar.wait_ge(sem["vWS"], j + 1)
                            scalar.dma_start(wstage[0][:, :],
                                             wffn_d[(j + 1) * 128:(j + 2) * 128, :]
                                             ).then_inc(sem["dWF1"], 16)

                    seed_tile(0)
                    for c in range(4):
                        xn1_chunk(0, c)
                    scalar.dma_start(wstage[0][:, :], wffn_d[0:128, :]
                                     ).then_inc(sem["dWF1"], 16)
                    seed_tile(1)
                    skphi_tile(0, stats_step, 0)
                    for c in range(4):
                        xn1_chunk(1, c)
                    seed_tile(2)
                    skphi_tile(1, stats_step, 4)
                    for c in range(4):
                        xn1_chunk(2, c)
                    seed_tile(3)
                    skphi_tile(2)
                    for c in range(4):
                        xn1_chunk(3, c)
                    skphi_tile(3)
                    for i in range(NTI):
                        sqphi_tile(i)

                @blk.tensor
                def _(tensor):
                    tensor.wait_ge(sem["vIN"], 1)
                    tensor.wait_ge(sem["gIN"], 2)

                    def tr_chunk(g):
                        tensor.wait_ge(sem["sSX"], g + 1)
                        for j in range(2):
                            q = 2 * g + j
                            if q > 1:
                                tensor.wait_ge(sem["vTE"], q - 1)
                            for k4 in range(4):
                                k = 4 * j + k4
                                tt = tensor.transpose(
                                    ptr[j][:, k4 * 128:(k4 + 1) * 128],
                                    xn1_tm[g % 2][:, k * 128:(k + 1) * 128],
                                    id_b[:, :],
                                )
                            tt.then_inc(sem["sTP"], 1)

                    tr_chunk(0)
                    tr_chunk(1)
                    for g in range(NCH):
                        tensor.wait_ge(sem["vTE"], 2 * g + 2)
                        if g > 0:
                            tensor.wait_ge(sem["sSK"], 2 * g)
                        else:
                            tensor.wait_ge(sem["dWg"], 144)
                        for s_ in range(2):
                            tensor.matmul(kps[:, s_ * 512:(s_ + 1) * 512],
                                          ones_row[0:1, :],
                                          bkv_row[0:1, s_ * 512:(s_ + 1) * 512],
                                          start=True, stop=False)
                        for k in range(8):
                            for s_ in range(2):
                                mm = tensor.matmul(
                                    kps[:, s_ * 512:(s_ + 1) * 512],
                                    xn1T_f[:, k, g * 128:(g + 1) * 128],
                                    wq_sb[:, k, E + s_ * 512:E + (s_ + 1) * 512],
                                    start=False, stop=(k == 7),
                                )
                        mm.then_inc(sem["sTK"], 1)
                        if g > 0:
                            tensor.wait_ge(sem["sVV"], g)
                        for s_ in range(2):
                            tensor.matmul(vps[:, s_ * 512:(s_ + 1) * 512],
                                          ones_row[0:1, :],
                                          bkv_row[0:1, E + s_ * 512:E + (s_ + 1) * 512],
                                          start=True, stop=False)
                        for k in range(8):
                            for s_ in range(2):
                                mm = tensor.matmul(
                                    vps[:, s_ * 512:(s_ + 1) * 512],
                                    xn1T_f[:, k, g * 128:(g + 1) * 128],
                                    wq_sb[:, k, 2 * E + s_ * 512:2 * E + (s_ + 1) * 512],
                                    start=False, stop=(k == 7),
                                )
                        mm.then_inc(sem["sTV"], 1)
                        if 2 <= g + 1 < NCH:
                            tr_chunk(g + 1)
                        tensor.wait_ge(sem["sVK"], g + 1)
                        tensor.wait_ge(sem["sVV"], g + 1)
                        for h in range(16):
                            half, slot = h % 2, h // 2
                            tensor.matmul(
                                kvps[64 * half:64 * half + 64, 64 * slot:64 * slot + 64],
                                pk_tm[:, h * 64:(h + 1) * 64],
                                v_tm[:, h * 64:(h + 1) * 64],
                                start=(g == 0 and slot == 0),
                                stop=(g == 15 and slot == 7),
                                tile_position=(0, 64) if half else None,
                                skip_group_check=True,
                            )
                        for p in range(8):
                            mm = tensor.matmul(
                                ksps[:, p:p + 1],
                                pk_tm[:, p * 128:(p + 1) * 128],
                                ones_cb[:, 0:1],
                                start=(g == 0 and p == 0),
                                stop=(g == 15 and p == 7),
                            )
                        mm.then_inc(sem["sTM2"], 1)
                    tensor.wait_ge(sem["dWgQ"], 128)
                    for i in range(NTI):
                        for m in range(8):
                            gm = 8 * i + m
                            if gm == 0:
                                tensor.wait_ge(sem["sSK"], 32)
                            if gm >= 2:
                                tensor.wait_ge(sem["sSQ"], 2 * (gm - 2) + 2)
                            for k in range(8):
                                mm = tensor.matmul(
                                    qps[m % 2][:, :],
                                    wq_sb[:, k, m * 128:(m + 1) * 128],
                                    xn1T_f[:, k, i * 512:(i + 1) * 512],
                                    start=(k == 0), stop=(k == 7),
                                )
                            mm.then_inc(sem["sTQ"], 1)

        with ExitStack() as PB:
            eb = PB.enter_context
            wqT = eb(nc.sbuf_tensor("wqT", [128, 8, E], BF16))
            _sg = eb(nc.sbuf_tensor("sg1", [128, E], BF16))
            sg = [wdum, _sg]
            wnum = eb(nc.sbuf_tensor("wnum", [128, E], BF16))
            wqc = [eb(nc.sbuf_tensor(f"wqc{j}", [128, E], BF16)) for j in range(2)]
            wstg = [wstage[0], wstage[0]]
            den_sb = eb(nc.sbuf_tensor("den_sb", [128, 1024], F32))
            xn2_sb = eb(nc.sbuf_tensor("xn2_sb", [128, NCH, E], F32))
            gamb = eb(nc.sbuf_tensor("gamb", [128, E], F32))
            betb = eb(nc.sbuf_tensor("betb", [128, E], F32))
            xq_tm = [eb(nc.sbuf_tensor(f"xq_tm{j}", [128, E], BF16)) for j in range(2)]
            m4s = [eb(nc.psum_tensor(f"m4s{j}", [128, 512], F32)) for j in range(2)]
            atn_ps = eb(nc.psum_tensor("atn_ps", [128, 1024], F32))
            den_ps = eb(nc.psum_tensor("den_ps", [128, 16], F32))
            ptc = [eb(nc.psum_tensor(f"ptc{j}", [128, 512], BF16)) for j in range(2)]

            with nc.Block() as blk:

                @blk.sync
                def _(sync):
                    sync.sem_inc(sem["bar1"], 1)
                    sync.wait_ge(sem["bar1"], 5)
                    sync.dma_start(gamb[:, :],
                                   bass.AP(gam_d, 0, [[0, 128], [1, E]])).then_inc(sem["dGB"], 16)
                    sync.dma_start(betb[:, :],
                                   bass.AP(bet_d, 0, [[0, 128], [1, E]])).then_inc(sem["dGB"], 16)
                    sync.wait_ge(sem["vMX"], 4)
                    sync.dma_start(cc_mx_i[0:1, 0:1], mxs[0:1, 0:1]).then_inc(sem["dMXI"], 16)
                    sync.wait_ge(sem["sCC2"], 1)
                    sync.dma_start(gmax_b[:, :],
                                   bass.AP(cc_mx_o, 0, [[0, 128], [1, 1]])
                                   ).then_inc(sem["dGM"], 16)
                    sync.wait_ge(sem["vAL"], 4)
                    sync.dma_start(scr[0:1, 1:2], alpha[0:1, 0:1]).then_inc(sem["dAB"], 16)
                    sync.wait_ge(sem["dAB"], 16)
                    sync.dma_start(alpha_b[:, :],
                                   bass.AP(scr, 1, [[0, 128], [1, 1]])
                                   ).then_inc(sem["dAB"], 16)
                    for go in range(NCH):
                        sync.wait_ge(sem["vEV"], 2 * (go + 1))
                        sync.dma_start(out_d[go * 128:(go + 1) * 128, :],
                                       r_sb[:, go, :]).then_inc(sem["dOUT"], 16)
                    sync.wait_ge(sem["dOUT"], 16 * NCH)
                    if debug:
                        sync.dma_start(dbg_kvloc.ap(), kv_full[:, :]).then_inc(sem["dOUT"], 16)
                        sync.dma_start(dbg_z.ap(), z_all[:, :, :]).then_inc(sem["dOUT"], 16)
                        sync.dma_start(dbg_sc[0:1, 0:1], gmax_b[0:1, :]).then_inc(sem["dOUT"], 16)
                        sync.dma_start(dbg_sc[0:1, 1:2], alpha[0:1, :]).then_inc(sem["dOUT"], 16)
                        sync.dma_start(dbg_sc[0:1, 2:4], scb[0:1, :]).then_inc(sem["dOUT"], 16)
                        sync.dma_start(dbg_sc[0:1, 4:5], mxs[0:1, :]).then_inc(sem["dOUT"], 16)
                        sync.dma_start(dbg_sc[0:1, 5:6], s1[0:1, :]).then_inc(sem["dOUT"], 16)
                        sync.wait_ge(sem["dOUT"], 16 * NCH + 7 * 16)

                @blk.gpsimd
                def _(gpsimd):
                    gpsimd.sem_inc(sem["bar1"], 1)
                    gpsimd.wait_ge(sem["bar1"], 5)
                    gpsimd.dma_start(wstg[0][:, :], wffn_d[0:128, :]
                                     ).then_inc(sem["dMK0"], 16)

                    def mask_iter(j):
                        gpsimd.wait_ge(sem["sSGN"], 2 * (j + 1))
                        if j >= 2:
                            gpsimd.wait_ge(sem["sACC"], 2 * (j - 1))
                            gpsimd.wait_ge(sem["sTPC"], 2 * (j - 1))
                        gpsimd.tensor_add(wqc[j % 2][:, :], sg[0][:, :],
                                          sg[1][:, :]).then_inc(sem["gMK"], 1)
                        if j >= 1:
                            gpsimd.wait_ge(sem["sACC"], 2 * j - 1)
                        gpsimd.wait_ge(sem["gMK"], 2 * j + 1)
                        gpsimd.tensor_mul(wnum[:, :], wstg[0][:, :],
                                          wqc[j % 2][:, :]).then_inc(sem["gMK"], 1)
                        if j + 1 < 8:
                            gpsimd.wait_ge(sem["gMK"], 2 * j + 2)
                            gpsimd.dma_start(wstg[0][:, :],
                                             wffn_d[(j + 1) * 128:(j + 2) * 128, :]
                                             ).then_inc(sem["dMK0"], 16)

                    for i in range(NTI):
                        mask_iter(2 * i)
                        mask_iter(2 * i + 1)
                        for c in range(4):
                            g = 4 * i + c
                            gpsimd.wait_ge(sem["sY2"], g + 1)
                            gpsimd.wait_ge(sem["dGB"], 32)
                            gpsimd.tensor_mul(xn2_sb[:, g, :], xn2_sb[:, g, :],
                                              gamb[:, :]).then_inc(sem["gGM"], 1)
                            gpsimd.wait_ge(sem["gGM"], 2 * g + 1)
                            gpsimd.tensor_add(xn2_sb[:, g, :], xn2_sb[:, g, :],
                                              betb[:, :]).then_inc(sem["gGM"], 1)
                    gpsimd.wait_ge(sem["vMX"], 2)
                    gpsimd.tensor_reduce(mxs[0:1, 0:1], mxcol[:, 0:1], AX.XYZWC,
                                         ALU.max).then_inc(sem["gMX"], 1)
                    gpsimd.wait_ge(sem["dMXI"], 16)
                    gpsimd.collective_compute(
                        "AllReduce", ALU.max,
                        replica_groups=[[0, 1, 2, 3, 4, 5, 6, 7]],
                        ins=[cc_mx_i.ap().opt()],
                        outs=[cc_mx_o.ap().opt()],
                    ).then_inc(sem["sCC2"], 1)
                    gpsimd.wait_ge(sem["sACC"], 16)
                    gpsimd.tensor_reduce(dnum[0:1, 0:1], numacc[:, :], AX.XYZWC,
                                         ALU.add).then_inc(sem["gRD2"], 1)
                    gpsimd.tensor_reduce(dden[0:1, 0:1], denacc[:, :], AX.XYZWC,
                                         ALU.add).then_inc(sem["gRD2"], 1)

                @blk.vector
                def _(vector):
                    vector.wait_ge(sem["sVQ"], 32)
                    vector.sem_inc(sem["bar1"], 1)
                    vector.wait_ge(sem["bar1"], 5)
                    vb = 0
                    vector.memset(bd[:, :, :], 0.0).then_inc(sem["vBD"], 1)
                    vector.memset(ksbd[:, :], 0.0).then_inc(sem["vBD"], 1)
                    vb += 2
                    vector.wait_ge(sem["gKV"], 1)
                    vector.wait_ge(sem["vBD"], vb)
                    vector.tensor_copy(
                        bass.AP(bd, 0, [[1024, 64], [128, 8], [1, 64]]),
                        bass.AP(kv_full, 0, [[520, 64], [64, 8], [1, 64]]),
                    ).then_inc(sem["vBD"], 1)
                    vector.tensor_copy(
                        bass.AP(bd, 64 * 1024 + 64, [[1024, 64], [128, 8], [1, 64]]),
                        bass.AP(kv_full, 64 * 520, [[520, 64], [64, 8], [1, 64]]),
                    ).then_inc(sem["vBD"], 1)
                    vector.tensor_copy(
                        bass.AP(ksbd, 0, [[16, 64], [2, 8]]),
                        bass.AP(kv_full, 512, [[520, 64], [1, 8]]),
                    ).then_inc(sem["vBD"], 1)
                    vector.tensor_copy(
                        bass.AP(ksbd, 64 * 16 + 1, [[16, 64], [2, 8]]),
                        bass.AP(kv_full, 64 * 520 + 512, [[520, 64], [1, 8]]),
                    ).then_inc(sem["vBD"], 1)
                    vb += 4
                    vz = 0
                    vsc = 0
                    cv = 0
                    nw2 = 0

                    for g in range(NCH):
                        i = g // 4
                        vector.wait_ge(sem["sTB"], g + 1)
                        vector.tensor_scalar_add(z_all[:, g, :], den_ps[:, :],
                                                 EPSZ).then_inc(sem["vZ"], 1)
                        vz += 1
                        vector.wait_ge(sem["vZ"], vz)
                        vector.reciprocal(z_all[:, g, :], z_all[:, g, :]).then_inc(sem["vZ"], 1)
                        vz += 1
                        vector.wait_ge(sem["vZ"], vz)
                        for h in range(16):
                            vector.scalar_tensor_tensor(
                                r_sb[:, g, h * 64:(h + 1) * 64],
                                atn_ps[:, h * 64:(h + 1) * 64],
                                z_all[:, g, h:h + 1],
                                r_sb[:, g, h * 64:(h + 1) * 64],
                                ALU.mult, ALU.add,
                            ).then_inc(sem["vSC"], 1)
                        vsc += 16
                        vector.wait_ge(sem["vSC"], vsc)
                        vector.bn_stats(st12[:, 0:6], r_sb[:, g, 0:512]).then_inc(sem["vL2"], 1)
                        vector.bn_stats(st12[:, 6:12], r_sb[:, g, 512:1024]).then_inc(sem["vL2"], 1)
                        cv += 2
                        vector.wait_ge(sem["vL2"], cv)
                        vector.bn_aggr(mvp[:, :], st12[:, :]).then_inc(sem["vL2"], 1)
                        cv += 1
                        vector.wait_ge(sem["vL2"], cv)
                        vector.tensor_copy(mu_all[:, g:g + 1], mvp[:, 0:1]).then_inc(sem["vL2"], 1)
                        vector.tensor_copy(var_all[:, g:g + 1], mvp[:, 1:2]).then_inc(sem["vL2"], 1)
                        cv += 2
                        if g >= 4:
                            gm = g - 4
                            vector.wait_ge(sem["gGM"], 2 * gm + 2)
                            vector.tensor_reduce(mxcols[:, gm:gm + 1], xn2_sb[:, gm, :],
                                                 AX.X, ALU.max, apply_absolute_value=True
                                                 ).then_inc(sem["vBB"], 1)
                        if g % 4 == 3:
                            vector.wait_ge(sem["vL2"], cv)
                            vector.tensor_scalar_add(ve4[:, :], var_all[:, 4 * i:4 * i + 4],
                                                     EPS).then_inc(sem["vNW2"], 1)
                            vector.wait_ge(sem["sL2"], 4 * i + 2)
                            vector.wait_ge(sem["vNW2"], nw2 + 1)
                            vector.tensor_mul(t4[:, :], y04[:, :], y04[:, :]).then_inc(sem["vNW2"], 1)
                            vector.wait_ge(sem["vNW2"], nw2 + 2)
                            vector.tensor_mul(t4[:, :], t4[:, :], ve4[:, :]).then_inc(sem["vNW2"], 1)
                            vector.wait_ge(sem["vNW2"], nw2 + 3)
                            vector.tensor_scalar(t4[:, :], t4[:, :], -0.5, 1.5,
                                                 ALU.mult, ALU.add).then_inc(sem["vNW2"], 1)
                            vector.wait_ge(sem["vNW2"], nw2 + 4)
                            vector.tensor_mul(t4[:, :], y04[:, :], t4[:, :]).then_inc(sem["vNW2"], 1)
                            vector.wait_ge(sem["vNW2"], nw2 + 5)
                            vector.tensor_mul(var2[:, :], t4[:, :], t4[:, :]).then_inc(sem["vNW2"], 1)
                            vector.wait_ge(sem["vNW2"], nw2 + 6)
                            vector.tensor_mul(var2[:, :], var2[:, :],
                                              var_all[:, 4 * i:4 * i + 4]).then_inc(sem["vNW2"], 1)
                            vector.wait_ge(sem["vNW2"], nw2 + 7)
                            vector.tensor_scalar_add(var2[:, :], var2[:, :],
                                                     EPS).then_inc(sem["vNW2"], 1)
                            nw2 += 8
                            vector.wait_ge(sem["sL2"], 4 * i + 4)
                            vector.tensor_mul(t4b[:, :], y04b[:, :], y04b[:, :]).then_inc(sem["vNW2"], 1)
                            vector.wait_ge(sem["vNW2"], nw2 + 1)
                            vector.tensor_mul(t4b[:, :], t4b[:, :], var2[:, :]).then_inc(sem["vNW2"], 1)
                            vector.wait_ge(sem["vNW2"], nw2 + 2)
                            vector.tensor_scalar(t4b[:, :], t4b[:, :], -0.5, 1.5,
                                                 ALU.mult, ALU.add).then_inc(sem["vNW2"], 1)
                            vector.wait_ge(sem["vNW2"], nw2 + 3)
                            vector.tensor_mul(t4b[:, :], y04b[:, :], t4b[:, :]).then_inc(sem["vNW2"], 1)
                            vector.wait_ge(sem["vNW2"], nw2 + 4)
                            vector.tensor_mul(s2_all[:, 4 * i:4 * i + 4], t4[:, :],
                                              t4b[:, :]).then_inc(sem["vNW2"], 1)
                            vector.wait_ge(sem["vNW2"], nw2 + 5)
                            vector.tensor_mul(nm_all[:, 4 * i:4 * i + 4],
                                              mu_all[:, 4 * i:4 * i + 4],
                                              s2_all[:, 4 * i:4 * i + 4]).then_inc(sem["vNW2"], 1)
                            vector.wait_ge(sem["vNW2"], nw2 + 6)
                            vector.tensor_scalar_mul(nm_all[:, 4 * i:4 * i + 4],
                                                     nm_all[:, 4 * i:4 * i + 4],
                                                     -1.0).then_inc(sem["vNW2"], 1)
                            nw2 += 7
                        if g % 2 == 1:
                            for q in (g - 1, g):
                                jj, b = q // 2, q % 2
                                vector.wait_ge(sem["sTPC"], q + 1)
                                vector.tensor_copy(
                                    bass.AP(wqT, (4 * b) * E + jj * 128,
                                            [[8 * E, 128], [E, 4], [1, 128]]),
                                    ptc[q % 2][:, :],
                                ).then_inc(sem["vTC"], 1)
                    for gm in range(NCH - 4, NCH):
                        vector.wait_ge(sem["gGM"], 2 * gm + 2)
                        vector.tensor_reduce(mxcols[:, gm:gm + 1], xn2_sb[:, gm, :],
                                             AX.X, ALU.max, apply_absolute_value=True
                                             ).then_inc(sem["vBB"], 1)
                    vector.wait_ge(sem["vBB"], NCH)
                    vector.tensor_reduce(mxcol[:, 0:1], mxcols[:, :], AX.X,
                                         ALU.max).then_inc(sem["vMX"], 1)
                    vector.sem_inc(sem["vMX"], 1)
                    vector.wait_ge(sem["gMX"], 1)
                    vector.sem_inc(sem["vMX"], 2)
                    vector.wait_ge(sem["gRD2"], 2)
                    vector.tensor_scalar_max(dden[0:1, 0:1], dden[0:1, 0:1], 1.0
                                             ).then_inc(sem["vAL"], 1)
                    vector.wait_ge(sem["vAL"], 1)
                    vector.reciprocal(dden[0:1, 0:1], dden[0:1, 0:1]).then_inc(sem["vAL"], 1)
                    vector.wait_ge(sem["vAL"], 2)
                    vector.tensor_mul(alpha[0:1, 0:1], dnum[0:1, 0:1],
                                      dden[0:1, 0:1]).then_inc(sem["vAL"], 1)
                    vector.wait_ge(sem["vAL"], 3)
                    vector.tensor_scalar_mul(alpha[0:1, 0:1], alpha[0:1, 0:1],
                                             0.5).then_inc(sem["vAL"], 1)
                    vector.wait_ge(sem["dGM"], 16)
                    vector.tensor_scalar_max(t_b[:, :], gmax_b[:, :], 1e-8
                                             ).then_inc(sem["vSCL"], 1)
                    vector.wait_ge(sem["vSCL"], 1)
                    vector.tensor_scalar_mul(t_b[:, :], t_b[:, :],
                                             1.0 / QMAX).then_inc(sem["vSCL"], 1)
                    vector.wait_ge(sem["vSCL"], 2)
                    vector.reciprocal(scb[:, 0:1], t_b[:, :]).then_inc(sem["vSCL"], 1)
                    vector.wait_ge(sem["dAB"], 32)
                    vector.tensor_mul(scb[:, 1:2], t_b[:, :],
                                      alpha_b[:, :]).then_inc(sem["vSCL"], 1)
                    vector.wait_ge(sem["vSCL"], 4)

                    def m4_evac(ge):
                        for h in range(2):
                            q = 2 * ge + h
                            vector.wait_ge(sem["sTM4"], q + 1)
                            vector.scalar_tensor_tensor(
                                r_sb[:, ge, h * 512:(h + 1) * 512],
                                m4s[q % 2][:, :], scb[:, 1:2],
                                r_sb[:, ge, h * 512:(h + 1) * 512],
                                ALU.mult, ALU.add,
                            ).then_inc(sem["vEV"], 1)

                    for g in range(NCH):
                        for b in range(2):
                            q = 16 + 2 * g + b
                            vector.wait_ge(sem["sTPC"], q + 1)
                            vector.tensor_copy(
                                bass.AP(actT, (4 * b) * T + g * 128,
                                        [[8 * T, 128], [T, 4], [1, 128]]),
                                ptc[q % 2][:, :],
                            ).then_inc(sem["vTC"], 1)
                        if g > 0:
                            m4_evac(g - 1)
                    m4_evac(NCH - 1)

                @blk.scalar
                def _(scalar):
                    scalar.wait_ge(sem["sSQ"], 64)
                    scalar.wait_ge(sem["sSX"], 16)
                    scalar.sem_inc(sem["bar1"], 1)
                    scalar.wait_ge(sem["bar1"], 5)

                    def sign_pair(j):
                        scalar.wait_ge(sem["dMK0"], 16 * (j + 1))
                        scalar.wait_ge(sem["vDL"], 2)
                        if j >= 1:
                            scalar.wait_ge(sem["gMK"], 2 * j - 1)
                        scalar.activation(sg[0][:, :], wstg[0][:, :], AF.Sign,
                                          bias=ndelta_c[:, 0:1]).then_inc(sem["sSGN"], 1)
                        scalar.activation(sg[1][:, :], wstg[0][:, :], AF.Sign,
                                          bias=delta_c[:, 0:1]).then_inc(sem["sSGN"], 1)

                    def accum_pair(j):
                        scalar.wait_ge(sem["gMK"], 2 * j + 2)
                        if j > 0:
                            scalar.wait_ge(sem["sACC"], 2 * j)
                        scalar.activation(den_sb[:, :], wnum[:, :], AF.Identity,
                                          accum_out=numacc[:, j:j + 1]
                                          ).then_inc(sem["sACC"], 1)
                        scalar.wait_ge(sem["sACC"], 2 * j + 1)
                        scalar.activation(den_sb[:, :], wqc[j % 2][:, :], AF.Abs,
                                          accum_out=denacc[:, j:j + 1]
                                          ).then_inc(sem["sACC"], 1)

                    sign_pair(0)
                    sign_pair(1)
                    for i in range(NTI):
                        scalar.wait_ge(sem["vL2"], 5 * (4 * i + 4))
                        if i > 0:
                            scalar.wait_ge(sem["vNW2"], 15 * i)
                        scalar.activation(y04[:, :], var_all[:, 4 * i:4 * i + 4],
                                          AF.Ln, bias=eps_c[:, 0:1]).then_inc(sem["sL2"], 1)
                        scalar.wait_ge(sem["sL2"], 4 * i + 1)
                        scalar.activation(y04[:, :], y04[:, :], AF.Exp,
                                          scale=-0.5).then_inc(sem["sL2"], 1)
                        scalar.wait_ge(sem["vNW2"], 15 * i + 8)
                        scalar.activation(y04b[:, :], var2[:, :], AF.Ln).then_inc(sem["sL2"], 1)
                        scalar.wait_ge(sem["sL2"], 4 * i + 3)
                        scalar.activation(y04b[:, :], y04b[:, :], AF.Exp,
                                          scale=-0.5).then_inc(sem["sL2"], 1)
                        for c in range(4):
                            g = 4 * i + c
                            scalar.wait_ge(sem["vNW2"], 15 * (i + 1))
                            scalar.activation(xn2_sb[:, g, :], r_sb[:, g, :], AF.Identity,
                                              bias=nm_all[:, g:g + 1],
                                              scale=s2_all[:, g:g + 1]).then_inc(sem["sY2"], 1)
                        accum_pair(2 * i)
                        accum_pair(2 * i + 1)
                        if i < 3:
                            sign_pair(2 * i + 2)
                            sign_pair(2 * i + 3)
                    for g in range(NCH):
                        scalar.wait_ge(sem["vSCL"], 3)
                        scalar.wait_ge(sem["vBB"], g + 1)
                        if g > 0:
                            scalar.wait_ge(sem["sRD"], 2 * g)
                        scalar.activation(den_sb[:, :], xn2_sb[:, g, :], AF.Identity,
                                          bias=rc_c[:, 0:1],
                                          scale=scb[:, 0:1]).then_inc(sem["sRD"], 1)
                        scalar.wait_ge(sem["sRD"], 2 * g + 1)
                        if g >= 2:
                            scalar.wait_ge(sem["sTPC"], 16 + 2 * (g - 1))
                        scalar.activation(xq_tm[g % 2][:, :], den_sb[:, :], AF.Identity,
                                          bias=nrc_c[:, 0:1]).then_inc(sem["sRD"], 1)

                @blk.tensor
                def _(tensor):
                    tensor.wait_ge(sem["sTQ"], 32)
                    tensor.sem_inc(sem["bar1"], 1)
                    tensor.wait_ge(sem["bar1"], 5)
                    tensor.wait_ge(sem["vBD"], 6)
                    for g in range(NCH):
                        if g > 0:
                            tensor.wait_ge(sem["vSC"], 16 * g)
                            tensor.wait_ge(sem["vZ"], 2 * g)
                        for p in range(8):
                            tensor.matmul(
                                atn_ps[:, p * 128:(p + 1) * 128],
                                actT[:, p, g * 128:(g + 1) * 128],
                                bd[:, p, :],
                                start=True, stop=True,
                            )
                            mm = tensor.matmul(
                                den_ps[:, 2 * p:2 * p + 2],
                                actT[:, p, g * 128:(g + 1) * 128],
                                ksbd[:, 2 * p:2 * p + 2],
                                start=True, stop=True,
                            )
                        mm.then_inc(sem["sTB"], 1)
                        if g % 2 == 1:
                            j = g // 2
                            tensor.wait_ge(sem["gMK"], 2 * j + 1)
                            for b in range(2):
                                q = 2 * j + b
                                if q > 1:
                                    tensor.wait_ge(sem["vTC"], q - 1)
                                for k4 in range(4):
                                    k = 4 * b + k4
                                    tt = tensor.transpose(
                                        ptc[q % 2][:, k4 * 128:(k4 + 1) * 128],
                                        wqc[j % 2][:, k * 128:(k + 1) * 128],
                                        id_b[:, :],
                                    )
                                tt.then_inc(sem["sTPC"], 1)
                    for g in range(NCH):
                        tensor.wait_ge(sem["sRD"], 2 * (g + 1))
                        for b in range(2):
                            q = 16 + 2 * g + b
                            tensor.wait_ge(sem["vTC"], q - 1)
                            for k4 in range(4):
                                k = 4 * b + k4
                                tt = tensor.transpose(
                                    ptc[q % 2][:, k4 * 128:(k4 + 1) * 128],
                                    xq_tm[g % 2][:, k * 128:(k + 1) * 128],
                                    id_b[:, :],
                                )
                            tt.then_inc(sem["sTPC"], 1)
                        tensor.wait_ge(sem["vTC"], 18 + 2 * g)
                        for s_ in range(2):
                            qq = 2 * g + s_
                            if qq >= 2:
                                tensor.wait_ge(sem["vEV"], qq - 1)
                            for k in range(8):
                                mm = tensor.matmul(
                                    m4s[qq % 2][:, :],
                                    actT[:, k, g * 128:(g + 1) * 128],
                                    wqT[:, k, s_ * 512:(s_ + 1) * 512],
                                    start=(k == 0), stop=(k == 7),
                                )
                            mm.then_inc(sem["sTM4"], 1)

    return nc


def kernel(L0_attn_x, L0_attn_W_qkv, L0_attn_b_qkv, L0_attn_W_proj,
           L0_attn_b_proj, L0_ffn_weight, L0_ffn_gamma, L0_ffn_beta):
    x = np.ascontiguousarray(np.asarray(L0_attn_x, dtype=np.float32)).reshape(B * S, E)
    wqkv = np.ascontiguousarray(np.asarray(L0_attn_W_qkv, dtype=np.float32))
    bqkv = np.ascontiguousarray(np.asarray(L0_attn_b_qkv, dtype=np.float32))
    wffn = np.ascontiguousarray(np.asarray(L0_ffn_weight, dtype=np.float32))
    gam = np.ascontiguousarray(np.asarray(L0_ffn_gamma, dtype=np.float32))
    bet = np.ascontiguousarray(np.asarray(L0_ffn_beta, dtype=np.float32))
    nc = build_nc()
    in_maps = [
        {"x": np.ascontiguousarray(x[c * T:(c + 1) * T]),
         "wqkv": wqkv, "bqkv": bqkv, "wffn": wffn, "gamma": gam, "beta": bet}
        for c in range(NC_)
    ]
    res = run_bass_kernel_spmd(nc, in_maps, core_ids=list(range(NC_)))
    out = np.concatenate([res.results[c]["out"] for c in range(NC_)], axis=0)
    return out.reshape(B, S, E)


if __name__ == "__main__":
    pass
